# version 20
# baseline (speedup 1.0000x reference)
import sys

sys.path.insert(0, "/opt/trn_rl_repo")

import numpy as np
import concourse.bass as bass
import concourse.mybir as mybir
from concourse.bass_utils import run_bass_kernel_spmd

F32 = mybir.dt.float32
BF16 = mybir.dt.bfloat16
I32 = mybir.dt.int32
AF = mybir.ActivationFunctionType
ALU = mybir.AluOpType
AX = mybir.AxisListType

D = 1024
NCH = 8
NHEAD = 16
NORM = 1.0 / (8.0 ** 0.5)
EPS_F = 1e-6
EPS_D = 1e-6
EPS_LN = 1e-5
GT = 512
NSUB = 4
RECIP_HILO = True
KSQ_GP = True
TR_ALT = True
HOIST_Q = True

ENGS = ("sync", "tensor", "vector", "scalar", "gpsimd")


class W:
    def __init__(self, sems):
        self.q = {e: [] for e in ENGS}
        self.c = {name: 0 for name in sems}
        self.sems = sems

    def em(self, e, f):
        self.q[e].append(f)

    def inc(self, e, f, sem, n=1):
        self.c[sem] += n
        s = self.sems[sem]

        def g(eng, f=f, s=s, n=n):
            f(eng).then_inc(s, n)

        self.q[e].append(g)
        return self.c[sem]

    def wait(self, e, sem, val):
        if val is None or val <= 0:
            return
        s = self.sems[sem]
        self.q[e].append(lambda eng, s=s, v=val: eng.wait_ge(s, v))

    def dma(self, e, f, sem):
        self.wait(e, sem, self.c[sem])
        return self.inc(e, f, sem, 16)


class Rot:

    def __init__(self, depth=2):
        self.hist = []
        self.depth = depth

    def acquire(self):
        i = len(self.hist)
        d = self.depth
        self.hist.append(None)
        guard = self.hist[i - d] if i >= d else None
        if i >= d:
            assert guard is not None, "psum slot acquired before prior release"
        return i, i % d, guard

    def release(self, idx, sem, val):
        self.hist[idx] = (sem, val)


def build_program(n_seq=4, n_grp=4, debug=False):
    seq_len = n_grp * GT
    ngroups = n_seq * n_grp

    nc = bass.Bass("TRN2", target_bir_lowering=False,
                   detect_race_conditions=False)

    x_d = nc.dram_tensor("x", [n_seq, seq_len, D], F32, kind="ExternalInput")
    wds = {}
    for nme in ("wq", "wk", "wv", "wo", "wr", "wk2", "wv2"):
        wds[nme] = nc.dram_tensor(nme, [D, D], F32, kind="ExternalInput")
    bq_d = nc.dram_tensor("bq", [D], F32, kind="ExternalInput")
    bk_d = nc.dram_tensor("bk", [D], F32, kind="ExternalInput")
    bv_d = nc.dram_tensor("bv", [D], F32, kind="ExternalInput")
    bo_d = nc.dram_tensor("bo", [D], F32, kind="ExternalInput")
    p_d = nc.dram_tensor("p", [64, 64], F32, kind="ExternalInput")
    mur_d = nc.dram_tensor("mur", [D], F32, kind="ExternalInput")
    gsel_d = nc.dram_tensor("gsel", [16, NCH, 128], F32, kind="ExternalInput")
    muk_d = nc.dram_tensor("muk", [D], F32, kind="ExternalInput")

    out_d = nc.dram_tensor("out", [n_seq, seq_len, D], F32,
                           kind="ExternalOutput")
    qfs_d = nc.dram_tensor("qfs", [ngroups, 128, NCH, GT], BF16,
                           kind="ExternalOutput" if debug else "Internal")

    al = nc.alloc_sbuf_tensor
    arena = al("arena", [128, 4, NCH, D], BF16)
    xin = al("xin", [128, 2, NSUB, D], F32)
    ln_tm = al("ln_tm", [128, 2, NSUB, D], BF16)
    lnT = al("lnT", [128, 2 * NCH, GT], BF16)
    qf_sb = al("qf_sb", [128, 2 * NCH, GT], BF16)
    kf_tm = al("kf_tm", [128, D], BF16)
    v_tm = al("v_tm", [128, D], BF16)
    kvbd = al("kvbd", [128, n_seq, NCH, 128], BF16)
    ksumb = al("ksumb", [128, n_seq, NCH, 16], BF16)
    attn = al("attn", [128, NCH, GT], BF16)
    xk_sb = al("xk_sb", [128, NCH, GT], BF16)
    ln2T = al("ln2T", [128, NCH, GT + 1], BF16)
    bnd = al("bnd", [128, NCH], BF16)
    r_sb = al("r_sb", [128, NSUB, D], BF16)
    stats = al("stats", [128, 2, NSUB, 8], F32)
    sqacc = al("sqacc", [128, 2, NSUB], F32)
    recTMe = al("recTMe", [128, NSUB * 16], F32)
    recTM = al("recTM", [128, NSUB * 16], F32)
    recTMh = al("recTMh", [128, NSUB * 16], BF16)
    recTMl = al("recTMl", [128, NSUB * 16], BF16)
    recT_h = al("recT_h", [16, GT], BF16)
    recT_l = al("recT_l", [16, GT], BF16)
    bc_sb = al("bc_sb", [128, 2, GT], F32)

    idx = al("idx", [128, 128], I32)
    ident = al("ident", [128, 128], BF16)
    identf = al("identf", [128, 128], F32)
    g16 = al("g16", [16, NCH, 128], BF16)
    ones_b = al("ones_b", [1, 128], BF16)
    ones_c = al("ones_c", [128, 1], BF16)
    ones4 = al("ones4", [128, NSUB], F32)
    zb = al("zb", [128, 128], BF16)
    eps5 = al("eps5", [128, 1], F32)
    h05 = al("h05", [128, 1], F32)
    mhalf = al("mhalf", [128, 1], F32)
    mone = al("mone", [128, 1], F32)
    ones_t = al("ones_t", [128, D], BF16)
    pf = al("pf", [64, 64], F32)
    pbf = al("pbf", [128, 128], F32)
    pbb = al("pbb", [128, 128], BF16)
    cqe = al("cqe", [128, NCH], F32)
    ckrow = al("ckrow", [1, D], BF16)
    bvrow = al("bvrow", [1, D], BF16)
    bcol = al("bcol", [128, NCH], F32)
    mur = al("mur_s", [128, NCH], F32)
    muk = al("muk_s", [128, NCH], F32)

    ap = nc.alloc_psum_tensor
    psA = ap("psA", [128, 512], F32)
    psB = ap("psB", [128, 512], F32)
    psC = ap("psC", [128, 1024], F32)
    psD = ap("psD", [128, 512], F32)
    psE = ap("psE", [128, 512], F32)
    psKV = ap("psKV", [128, 512], F32)
    psKS = ap("psKS", [128, 128], F32)

    sems = {n: nc.alloc_semaphore(n) for n in
            ("sX", "sW", "sQ", "sG", "sO", "sT", "sV", "sS", "sP", "sB")}
    w = W(sems)
    ev = {}
    rotDE = Rot(4)
    rotAB = rotDE
    guardC = {0: None, 1: None}
    _ps_pool = None

    def ps_de(slot):
        return (psD, psE, psA, psB)[slot]

    def ps_ab(slot):
        return (psD, psE, psA, psB)[slot]

    with nc.Block() as blk:

        @blk.sync
        def _(eng):
            eng.dma_start(pf[:, :], p_d[:, :]).then_inc(sems["sX"], 16)
            eng.dma_start(
                bcol[:, :], bq_d[:].rearrange("(j p) -> p j", p=128),
                allow_slow_non_contiguous=True).then_inc(sems["sX"], 16)
            eng.dma_start(
                mur[:, :], mur_d[:].rearrange("(j p) -> p j", p=128),
                allow_slow_non_contiguous=True).then_inc(sems["sX"], 16)
            eng.dma_start(
                muk[:, :], muk_d[:].rearrange("(j p) -> p j", p=128),
                allow_slow_non_contiguous=True).then_inc(sems["sX"], 16)
            eng.wait_ge(sems["sX"], 16 * 4)

        @blk.gpsimd
        def _(eng):
            eng.iota(idx[:, :], [[1, 128]], channel_multiplier=-1)
            eng.memset(ones_b[:, :], 1.0)
            eng.memset(ones_c[:, :], 1.0)
            eng.memset(ones4[:, :], 1.0)
            eng.memset(zb[:, :], 0.0)
            eng.memset(eps5[:, :], EPS_LN)
            eng.memset(h05[:, :], 0.5)
            eng.memset(mhalf[:, :], -0.5)
            eng.memset(mone[:, :], -1.0)
            eng.memset(ones_t[:, :], 1.0)
            eng.memset(pbf[:, :], 0.0)
            eng.memset(kvbd[:, :, :, :], 0.0)
            eng.memset(ksumb[:, :, :, :], 0.0)
            eng.memset(stats[:, :, :, :], 0.0)
            eng.dma_start(g16[:, :, :], gsel_d[:, :, :]).then_inc(
                sems["sB"], 16)
            eng.wait_ge(sems["sB"], 16)

    with nc.Block() as blk:

        @blk.vector
        def _(eng):
            eng.tensor_scalar(out=ident[:, :], in0=idx[:, :], scalar1=0.0,
                              scalar2=None, op0=ALU.is_equal)
            eng.tensor_scalar(out=identf[:, :], in0=idx[:, :], scalar1=0.0,
                              scalar2=None, op0=ALU.is_equal)

    with nc.Block() as blk:

        @blk.tensor
        def _(eng):
            eng.matmul(psA[0:64, 0:64], pf[:, :], identf[0:64, 0:64],
                       start=True, stop=True)

    with nc.Block() as blk:

        @blk.vector
        def _(eng):
            eng.tensor_scalar(out=pbf[0:64, 0:64], in0=psA[0:64, 0:64],
                              scalar1=NORM, scalar2=None, op0=ALU.mult)
            eng.tensor_scalar(out=pbf[64:128, 64:128], in0=psA[0:64, 0:64],
                              scalar1=NORM, scalar2=None, op0=ALU.mult)

    with nc.Block() as blk:

        @blk.scalar
        def _(eng):
            eng.copy(pbb[:, :], pbf[:, :])

    with nc.Block() as blk:

        @blk.tensor
        def _(eng):
            for dc in range(NCH):
                eng.matmul(psKS[:, dc:dc + 1], pbf[:, :], bcol[:, dc:dc + 1],
                           start=True, stop=True)

    with nc.Block() as blk:

        @blk.vector
        def _(eng):
            eng.tensor_scalar(out=cqe[:, :], in0=psKS[:, 0:NCH],
                              scalar1=EPS_F, scalar2=None, op0=ALU.add)

        @blk.sync
        def _(eng):
            eng.dma_start(
                bcol[:, :], bk_d[:].rearrange("(j p) -> p j", p=128),
                allow_slow_non_contiguous=True).then_inc(sems["sX"], 16)
            eng.wait_ge(sems["sX"], 16 * 5)

    with nc.Block() as blk:

        @blk.tensor
        def _(eng):
            for dc in range(NCH):
                eng.matmul(psC[0:1, dc * 128:(dc + 1) * 128],
                           bcol[:, dc:dc + 1], pbf[:, :],
                           start=True, stop=True)

    with nc.Block() as blk:

        @blk.scalar
        def _(eng):
            eng.copy(ckrow[:, :], psC[0:1, :])

        @blk.gpsimd
        def _(eng):
            eng.dma_start(bvrow[:, :], bv_d[:].unsqueeze(0)).then_inc(
                sems["sB"], 16)
            eng.wait_ge(sems["sB"], 32)

    w.c["sX"] = 16 * 5

    def prep_weight(eid, arena_slot, wd, fuse_p, buf,
                    dma_waits=(), cast_waits=(), scratch=None,
                    evac_scale=None):
        if scratch is None:
            scratch = ln_tm[:, 0, :, :].rearrange("p a b -> p (a b)")
        skey = ("prep_trmm", scratch.tensor.name)
        sview = scratch.rearrange("p (a b) -> p a b", a=NSUB)
        dsem = "sX" if eid % 2 == 0 else "sW"
        ceng = "vector" if scratch.tensor.name == "attn" else "scalar"
        csem = "sV" if ceng == "vector" else "sS"
        for half in range(2):
            def ld(eng, wd=wd, half=half, buf=buf):
                return eng.dma_start(
                    xin[:, buf, :, :],
                    wd[:, :].rearrange("(dc p) c -> p dc c", p=128)[
                        :, 4 * half:4 * half + 4, :])
            if half == 0:
                for (e_sem, e_val) in dma_waits:
                    w.wait("sync", e_sem, e_val)
            else:
                w.wait("sync", csem, vc)
            vx = w.dma("sync", ld, dsem)
            w.wait(ceng, dsem, vx)
            if half == 0:
                for (e_sem, e_val) in cast_waits:
                    w.wait(ceng, e_sem, e_val)
            if ev.get(skey):
                w.wait(ceng, "sT", ev[skey])
            if ceng == "vector":
                vc = w.inc(ceng, lambda eng, buf=buf, scratch=scratch:
                           eng.tensor_copy(scratch,
                                           xin[:, buf, :, :].rearrange(
                                               "p a b -> p (a b)")), csem)
            else:
                vc = w.inc(ceng, lambda eng, buf=buf, scratch=scratch:
                           eng.copy(scratch,
                                    xin[:, buf, :, :].rearrange(
                                        "p a b -> p (a b)")), csem)
            ev[("wprep_cast", eid, half)] = (csem, vc)
            w.wait("tensor", csem, vc)
            for cc in range(NCH):
                ridx, slot, guard = rotDE.acquire()
                if guard:
                    w.wait("tensor", guard[0], guard[1])
                ps = ps_de(slot)
                for dc in range(4):
                    def mm(eng, ps=ps, dc=dc, cc=cc, fuse_p=fuse_p,
                           sview=sview):
                        return eng.matmul(
                            ps[:, dc * 128:(dc + 1) * 128],
                            sview[:, dc, cc * 128:(cc + 1) * 128],
                            pbb[:, :] if fuse_p else ident[:, :],
                            start=True, stop=True)
                    if dc == 3:
                        vt = w.inc("tensor", mm, "sT")
                    else:
                        w.em("tensor", mm)
                ee = "scalar" if (cc % 2 == 0 and evac_scale is None) \
                    else "vector"
                esem = "sS" if ee == "scalar" else "sV"
                w.wait(ee, "sT", vt)
                if evac_scale is not None:
                    def evac(eng, ps=ps, slot_=arena_slot, cc=cc, half=half,
                             sc_=evac_scale):
                        return eng.tensor_scalar(
                            out=arena[:, slot_, cc,
                                      half * 512:(half + 1) * 512],
                            in0=ps[:, :], scalar1=sc_, scalar2=None,
                            op0=ALU.mult)
                elif ee == "scalar":
                    def evac(eng, ps=ps, slot_=arena_slot, cc=cc, half=half):
                        return eng.copy(
                            arena[:, slot_, cc,
                                  half * 512:(half + 1) * 512], ps[:, :])
                else:
                    def evac(eng, ps=ps, slot_=arena_slot, cc=cc, half=half):
                        return eng.tensor_copy(
                            arena[:, slot_, cc,
                                  half * 512:(half + 1) * 512], ps[:, :])
                vse = w.inc(ee, evac, esem)
                rotDE.release(ridx, esem, vse)
                if ee == "scalar":
                    vse_s = vse
            ev[skey] = vt
        if evac_scale is not None:
            ev[("wprep", eid)] = ("sV", vse)
        else:
            ev[("wprep", eid)] = vse_s

    def emit_ln(g, apply_key, extra_stat_wait=None, apply_guard=None,
                stat_waits=None):
        b = g % 2
        sb_ = g % 2
        prev = ev.get((apply_key, g - 2))
        if prev is not None:
            w.wait("vector", "sV", prev)
            w.wait("scalar", "sV", prev)
        if extra_stat_wait is not None:
            w.wait("vector", extra_stat_wait[0], extra_stat_wait[1])
            w.wait("scalar", extra_stat_wait[0], extra_stat_wait[1])
        if apply_guard is not None:
            w.wait("scalar", apply_guard[0], apply_guard[1])
        w.em("scalar", lambda eng, sb_=sb_: eng.memzero(sqacc[:, sb_, :]))
        vred = w.inc("vector", lambda eng, sb_=sb_, b=b: eng.reduce_sum(
            stats[:, sb_, :, 0:1], xin[:, b, :, :], axis=AX.X), "sV")
        w.wait("vector", "sV", vred)
        for s_ in range(NSUB):
            vsq = w.inc("scalar", lambda eng, sb_=sb_, b=b, s_=s_: eng.activation(
                ln_tm[:, b, s_, :], xin[:, b, s_, :], AF.Square,
                accum_out=sqacc[:, sb_, s_:s_ + 1]), "sS")
        w.wait("vector", "sS", vsq)
        w.em("vector", lambda eng, sb_=sb_: eng.tensor_scalar(
            out=stats[:, sb_, :, 2:3], in0=stats[:, sb_, :, 0:1],
            scalar1=1.0 / D, scalar2=None, op0=ALU.mult))
        w.em("vector", lambda eng, sb_=sb_: eng.tensor_tensor(
            out=stats[:, sb_, :, 3:4], in0=stats[:, sb_, :, 2:3],
            in1=stats[:, sb_, :, 2:3], op=ALU.mult))
        vvar = w.inc("vector", lambda eng, sb_=sb_: eng.scalar_tensor_tensor(
            out=stats[:, sb_, :, 4:5], in0=sqacc[:, sb_, :].unsqueeze(-1),
            scalar=1.0 / D, in1=stats[:, sb_, :, 3:4],
            op0=ALU.mult, op1=ALU.subtract), "sV")
        w.wait("scalar", "sV", vvar)
        vstd = w.inc("scalar", lambda eng, sb_=sb_: eng.activation(
            stats[:, sb_, :, 5:6], stats[:, sb_, :, 4:5],
            AF.Sqrt, bias=eps5[:, :]), "sS")
        w.wait("vector", "sS", vstd)
        w.em("vector", lambda eng, sb_=sb_: eng.reciprocal(
            stats[:, sb_, :, 6:7], stats[:, sb_, :, 5:6]))
        if apply_guard is not None:
            w.wait("vector", apply_guard[0], apply_guard[1])
        for s_ in range(NSUB):
            vap = w.inc("vector", lambda eng, sb_=sb_, b=b, s_=s_:
                        eng.tensor_scalar(
                            out=ln_tm[:, b, s_, :], in0=xin[:, b, s_, :],
                            scalar1=stats[:, sb_, s_, 2:3],
                            scalar2=stats[:, sb_, s_, 6:7],
                            op0=ALU.subtract, op1=ALU.mult), "sV")
        ev[(apply_key, g)] = vap
        return vap

    def emit_transpose(g, dstT_base, evac_dst_fn, apply_key,
                       first_evac_waits=()):
        b = g % 2
        w.wait("tensor", "sV", ev[(apply_key, g)])
        last = {"scalar": None, "vector": None}
        for cc in range(NCH):
            ridx, slot, guard = rotDE.acquire()
            if guard:
                w.wait("tensor", guard[0], guard[1])
            ps = ps_de(slot)
            for s_ in range(NSUB):
                def mm(eng, ps=ps, b=b, s_=s_, cc=cc):
                    return eng.matmul(
                        ps[:, s_ * 128:(s_ + 1) * 128],
                        ln_tm[:, b, s_, cc * 128:(cc + 1) * 128],
                        ident[:, :], start=True, stop=True)
                if s_ == NSUB - 1:
                    vt = w.inc("tensor", mm, "sT")
                else:
                    w.em("tensor", mm)
            ev[("tr_mm", apply_key, g)] = vt
            e = "scalar" if (cc % 2 == 0 or not TR_ALT) else "vector"
            sem = "sS" if e == "scalar" else "sV"
            w.wait(e, "sT", vt)
            if cc in (0, 1):
                for (e_sem, e_val) in first_evac_waits:
                    w.wait(e, e_sem, e_val)
            vse = w.inc(e, lambda eng, ps=ps, cc=cc, v=(e == "vector"):
                        evac_dst_fn(eng, ps, cc, ve=v), sem)
            rotDE.release(ridx, sem, vse)
            last[e] = vse
        return last["scalar"], last["vector"]

    def a_xload(g):
        s, gi = divmod(g, n_grp)
        if g >= 2:
            w.wait("sync", "sV", ev[("lnA", g - 2)])
        elif g == 0:
            w.wait("sync", *ev[("wprep_cast", 2, 1)])
        elif g == 1:
            w.wait("sync", *ev[("wprep_cast", 3, 1)])

        def ld(eng, s=s, gi=gi, g=g):
            return eng.dma_start(
                xin[:, g % 2, :, :],
                x_d[s, gi * GT:(gi + 1) * GT, :].rearrange(
                    "(n p) c -> p n c", p=128))
        ev[("xloadA", g)] = w.dma("sync", ld, "sX")

    def emit_ln_a(g):
        apply_guard = None
        if g == 0:
            apply_guard = ("sS", ev[("wprep", 3)])
        elif ("tr_mm", "lnA", g - 2) in ev:
            apply_guard = ("sT", ev[("tr_mm", "lnA", g - 2)])
        emit_ln(g, "lnA", extra_stat_wait=("sX", ev[("xloadA", g)]),
                apply_guard=apply_guard)

    def ln_a_stats_pieces(g):
        b = g % 2
        sb_ = g % 2
        prev = ev.get(("lnA", g - 2))
        guard = ev[("tr_mm", "lnA", g - 2)]
        box = {}

        def p0():
            if prev is not None:
                w.wait("vector", "sV", prev)
                w.wait("scalar", "sV", prev)
            w.wait("vector", "sX", ev[("xloadA", g)])
            w.wait("scalar", "sX", ev[("xloadA", g)])
            w.wait("scalar", "sT", guard)
            w.em("scalar", lambda eng: eng.memzero(sqacc[:, sb_, :]))
            box["red"] = w.inc("vector", lambda eng: eng.reduce_sum(
                stats[:, sb_, :, 0:1], xin[:, b, :, :], axis=AX.X), "sV")

        def sq(lo, hi):
            def f():
                for s_ in range(lo, hi):
                    box["sq"] = w.inc(
                        "scalar", lambda eng, s_=s_: eng.activation(
                            ln_tm[:, b, s_, :], xin[:, b, s_, :], AF.Square,
                            accum_out=sqacc[:, sb_, s_:s_ + 1]), "sS")
            return f

        def p3():
            w.wait("vector", "sV", box["red"])
            w.wait("vector", "sS", box["sq"])
            w.em("vector", lambda eng: eng.tensor_scalar(
                out=stats[:, sb_, :, 2:3], in0=stats[:, sb_, :, 0:1],
                scalar1=1.0 / D, scalar2=None, op0=ALU.mult))
            w.em("vector", lambda eng: eng.tensor_tensor(
                out=stats[:, sb_, :, 3:4], in0=stats[:, sb_, :, 2:3],
                in1=stats[:, sb_, :, 2:3], op=ALU.mult))
            vvar = w.inc("vector", lambda eng: eng.scalar_tensor_tensor(
                out=stats[:, sb_, :, 4:5], in0=sqacc[:, sb_, :].unsqueeze(-1),
                scalar=1.0 / D, in1=stats[:, sb_, :, 3:4],
                op0=ALU.mult, op1=ALU.subtract), "sV")
            w.wait("scalar", "sV", vvar)
            vstd = w.inc("scalar", lambda eng: eng.activation(
                stats[:, sb_, :, 5:6], stats[:, sb_, :, 4:5],
                AF.Sqrt, bias=eps5[:, :]), "sS")
            w.wait("vector", "sS", vstd)
            w.em("vector", lambda eng: eng.reciprocal(
                stats[:, sb_, :, 6:7], stats[:, sb_, :, 5:6]))
        return [p0, sq(0, 2), sq(2, 4), p3]

    def ln_a_apply_piece(g, s_):
        b = g % 2
        sb_ = g % 2
        if s_ == 0:
            w.wait("vector", "sT", ev[("tr_mm", "lnA", g - 2)])
        vap = w.inc("vector", lambda eng: eng.tensor_scalar(
            out=ln_tm[:, b, s_, :], in0=xin[:, b, s_, :],
            scalar1=stats[:, sb_, s_, 2:3],
            scalar2=stats[:, sb_, s_, 6:7],
            op0=ALU.subtract, op1=ALU.mult), "sV")
        if s_ == NSUB - 1:
            ev[("lnA", g)] = vap

    def phase_a_group(g):
        s, gi = divmod(g, n_grp)
        b = g % 2

        def lnT_evac(eng, ps, cc, b=b, ve=False):
            if ve:
                return eng.tensor_copy(lnT[:, b * NCH + cc, :], ps[:, :])
            return eng.copy(lnT[:, b * NCH + cc, :], ps[:, :])
        few = []
        if ("kvacc", g - 2, NSUB - 1) in ev:
            few.append(("sT", ev[("kvacc", g - 2, NSUB - 1)]))
        vtr_s, vtr_v = emit_transpose(g, b, lnT_evac, "lnA",
                                      first_evac_waits=few)
        if debug and g == 1:
            dstat = nc.dram_tensor("dbg_stats_g1", [128, NSUB, 8], F32,
                                   kind="ExternalOutput")
            w.wait("gpsimd", "sP", ev[("lnA", 1)])
            w.dma("gpsimd", lambda eng, dstat=dstat: eng.dma_start(
                dstat[:, :, :], stats[:, 1, :, :]), "sO")
            w.wait("gpsimd", "sS", vtr_s)
            if vtr_v:
                w.wait("gpsimd", "sV", vtr_v)
            dsnap = nc.dram_tensor("dbg_lnT_g1", [128, 2 * NCH, GT], BF16,
                                   kind="ExternalOutput")
            w.dma("gpsimd", lambda eng, dsnap=dsnap: eng.dma_start(
                dsnap[:, :, :], lnT[:, :, :]), "sO")
        if g + 2 < ngroups:
            a_xload(g + 2)
        apply_next = (g + 1 < ngroups and ("lnA", g + 1) not in ev)
        w.wait("tensor", "sS", vtr_s)
        w.wait("tensor", "sV", vtr_v)
        if g == 0:
            w.wait("tensor", "sS", ev[("wprep", 0)])
        for oc in range(NCH):
            ridx, slot, guard = rotAB.acquire()
            if guard:
                w.wait("tensor", guard[0], guard[1])
            ps = ps_ab(slot)
            for cc in range(NCH):
                def mm(eng, ps=ps, oc=oc, cc=cc, b=b):
                    return eng.matmul(
                        ps[:, :], arena[:, 0, cc, oc * 128:(oc + 1) * 128],
                        lnT[:, b * NCH + cc, :],
                        start=(cc == 0), stop=(cc == NCH - 1))
                if cc == NCH - 1:
                    vt = w.inc("tensor", mm, "sT")
                else:
                    w.em("tensor", mm)
            w.wait("vector", "sT", vt)
            if g >= 2 and oc == 0:
                w.wait("vector", "sQ", 16 * (g - 1))

            def evac(eng, ps=ps, oc=oc, b=b):
                return eng.tensor_scalar(
                    out=qf_sb[:, b * NCH + oc, :], in0=ps[:, :],
                    scalar1=cqe[:, oc:oc + 1], scalar2=EPS_F,
                    op0=ALU.add, op1=ALU.max)
            vqe = w.inc("vector", evac, "sV")
            rotAB.release(ridx, "sV", vqe)
            if apply_next and 2 <= oc <= 5:
                ln_a_apply_piece(g + 1, oc - 2)
        ev[("qf_evac", g)] = vqe
        w.wait("gpsimd", "sV", vqe)
        w.dma("gpsimd", lambda eng, g=g, b=b: eng.dma_start(
            qfs_d[g, :, :, :], qf_sb[:, b * NCH:(b + 1) * NCH, :]), "sQ")

        ln_pieces = ln_a_stats_pieces(g + 2) if g + 2 < ngroups else None
        if g == 0:
            w.wait("tensor", "sS", ev[("wprep", 1)])
            w.wait("tensor", "sS", ev[("wprep", 2)])
        if gi == 0 and ("kvbd_evac", s - 1) in ev:
            w.wait("tensor", "sS", ev[("kvbd_evac", s - 1)])
        for sub in range(NSUB):
            nglob = gi * NSUB + sub
            for which, slot_, brow_ in (("k", 1, ckrow), ("v", 2, bvrow)):
                for half in range(2):
                    if guardC[half]:
                        w.wait("tensor", guardC[half][0], guardC[half][1])
                    hs = slice(half * 512, (half + 1) * 512)
                    for cc in range(NCH):
                        def mm(eng, hs=hs, cc=cc, b=b, sub=sub, slot_=slot_):
                            return eng.matmul(
                                psC[:, hs],
                                lnT[:, b * NCH + cc,
                                    sub * 128:(sub + 1) * 128],
                                arena[:, slot_, cc, hs],
                                start=(cc == 0), stop=False)
                        w.em("tensor", mm)

                    def mmb(eng, hs=hs, brow_=brow_):
                        return eng.matmul(
                            psC[:, hs], ones_b[:, :], brow_[:, hs],
                            start=False, stop=True)
                    vt = w.inc("tensor", mmb, "sT")
                    if which == "k":
                        w.wait("vector", "sT", vt)
                        if ("kvacc", g, sub - 1) in ev:
                            w.wait("vector", "sT", ev[("kvacc", g, sub - 1)])
                        elif sub == 0 and ("kvacc", g - 1, NSUB - 1) in ev:
                            w.wait("vector", "sT",
                                   ev[("kvacc", g - 1, NSUB - 1)])

                        def evac(eng, hs=hs):
                            return eng.tensor_scalar(
                                out=kf_tm[:, hs], in0=psC[:, hs],
                                scalar1=EPS_F, scalar2=EPS_F,
                                op0=ALU.add, op1=ALU.max)
                        vv = w.inc("vector", evac, "sV")
                        guardC[half] = ("sV", vv)
                    else:
                        w.wait("scalar", "sT", vt)
                        if ("kvacc", g, sub - 1) in ev:
                            w.wait("scalar", "sT", ev[("kvacc", g, sub - 1)])
                        elif sub == 0 and ("kvacc", g - 1, NSUB - 1) in ev:
                            w.wait("scalar", "sT",
                                   ev[("kvacc", g - 1, NSUB - 1)])

                        def evac(eng, hs=hs):
                            return eng.copy(v_tm[:, hs], psC[:, hs])
                        vv = w.inc("scalar", evac, "sS")
                        guardC[half] = ("sS", vv)
                ev[("kvsrc", g, sub, which)] = vv
                if which == "k" and ln_pieces is not None:
                    ln_pieces[sub]()
            w.wait("tensor", "sV", ev[("kvsrc", g, sub, "k")])
            w.wait("tensor", "sS", ev[("kvsrc", g, sub, "v")])
            first = nglob == 0
            last = nglob == n_grp * NSUB - 1
            if first:
                w.em("tensor", lambda eng, b=b: eng.matmul(
                    psKV[:, :], zb[:, :], lnT[:, b * NCH, :],
                    start=True, stop=False, skip_group_check=True))
                w.em("tensor", lambda eng, b=b: eng.matmul(
                    psKS[:, 0:NCH], zb[:, :], lnT[:, b * NCH, 0:NCH],
                    start=True, stop=False, skip_group_check=True))
            for h in range(NHEAD):
                par, j = h % 2, h // 2

                def mm(eng, h=h, par=par, j=j, last=last):
                    return eng.matmul(
                        psKV[64 * par:64 * par + 64, j * 64:(j + 1) * 64],
                        kf_tm[:, h * 64:(h + 1) * 64],
                        v_tm[:, h * 64:(h + 1) * 64],
                        start=False, stop=last, skip_group_check=True,
                        tile_position=(0, 64 * par))
                w.em("tensor", mm)
            for cc in range(NCH):
                def mm(eng, cc=cc, last=last):
                    return eng.matmul(
                        psKS[:, cc:cc + 1],
                        kf_tm[:, cc * 128:(cc + 1) * 128],
                        ones_c[:, :],
                        start=False, stop=last, skip_group_check=True)
                if cc == NCH - 1:
                    ev[("kvacc", g, sub)] = w.inc("tensor", mm, "sT")
                else:
                    w.em("tensor", mm)

        if gi == n_grp - 1:
            w.wait("scalar", "sT", ev[("kvacc", g, NSUB - 1)])
            for h in range(NHEAD):
                par, j = h % 2, h // 2
                rows = slice(64 * par, 64 * par + 64)

                def evac(eng, rows=rows, s=s, j=j, par=par):
                    return eng.copy(
                        kvbd[rows, s, j, 64 * par:64 * par + 64],
                        psKV[rows, j * 64:(j + 1) * 64])
                w.em("scalar", evac)
            for oc in range(NCH):
                def e1(eng, s=s, oc=oc):
                    return eng.copy(ksumb[0:64, s, oc, 2 * oc:2 * oc + 1],
                                    psKS[0:64, oc:oc + 1])

                def e2(eng, s=s, oc=oc):
                    return eng.copy(
                        ksumb[64:128, s, oc, 2 * oc + 1:2 * oc + 2],
                        psKS[64:128, oc:oc + 1])
                w.em("scalar", e1)
                if oc == NCH - 1:
                    ev[("kvbd_evac", s)] = w.inc("scalar", e2, "sS")
                else:
                    w.em("scalar", e2)

    def b_xload(g):
        s, gi = divmod(g, n_grp)
        if g == 0:
            w.wait("sync", *ev[("wprep_cast", 6, 1)])
        elif g == 1:
            w.wait("sync", *ev[("wprep_cast", 7, 1)])
        for sub in range(NSUB):
            if g >= 2:
                w.wait("sync", "sO", 16 * (NSUB * (g - 2) + sub + 1))

            def ld(eng, s=s, gi=gi, g=g, sub=sub):
                return eng.dma_start(
                    xin[:, g % 2, sub, :],
                    x_d[s, gi * GT + sub * 128:gi * GT + (sub + 1) * 128,
                        :])
            ev[("xloadB", g, sub)] = w.dma("sync", ld, "sX")
            w.wait("gpsimd", "sX", ev[("xloadB", g, sub)])

            def boacc(eng, g=g, sub=sub):
                return eng.dma_start(
                    xin[:, g % 2, sub, :],
                    bo_d[:].unsqueeze(0).broadcast_to([128, D]),
                    accum_op=ALU.add)
            ev[("boacc", g, sub)] = w.dma("gpsimd", boacc, "sB")

    def b_qfload(g):
        if ("r_done", g - 2) in ev:
            w.wait("gpsimd", "sT", ev[("r_done", g - 2)])
        w.wait("gpsimd", "sQ", 16 * (g + 1))

        def ld(eng, g=g):
            return eng.dma_start(
                qf_sb[:, (g % 2) * NCH:(g % 2 + 1) * NCH, :],
                qfs_d[g, :, :, :])
        ev[("qf_load", g)] = w.dma("gpsimd", ld, "sG")

    def dview_of(g):
        return ln_tm[:, g % 2, :, :].rearrange("p a b -> p (a b)").rearrange(
            "p (c d) -> p c d", c=NCH)

    def xkb(g):
        return xk_sb if g % 2 == 0 else lnT

    def b_att(g):
        s, gi = divmod(g, n_grp)
        b = g % 2

        w.wait("tensor", "sG", ev[("qf_load", g)])
        if g == 0:
            w.wait("tensor", "sS", ev[("kvbd_evac", n_seq - 1)])
        if ("recip_eps", g - 1) in ev:
            w.wait("tensor", "sV", ev[("recip_eps", g - 1)])
        for sub in range(NSUB):
            sl = slice(sub * 16, sub * 16 + 16)
            for oc in range(NCH):
                def mm(eng, sl=sl, oc=oc, b=b, sub=sub, s=s):
                    return eng.matmul(
                        psKS[:, sl],
                        qf_sb[:, b * NCH + oc, sub * 128:(sub + 1) * 128],
                        ksumb[:, s, oc, :],
                        start=(oc == 0), stop=(oc == NCH - 1))
                if oc == NCH - 1 and sub == NSUB - 1:
                    vdn = w.inc("tensor", mm, "sT")
                else:
                    w.em("tensor", mm)
        w.wait("vector", "sT", vdn)
        ev[("recip_eps", g)] = w.inc("vector", lambda eng: eng.tensor_scalar(
            out=recTMe[:, :], in0=psKS[:, 0:NSUB * 16], scalar1=EPS_D,
            scalar2=None, op0=ALU.add), "sV")
        vrec = w.inc("vector", lambda eng: eng.reciprocal(
            recTM[:, :], recTMe[:, :]), "sV")
        vh_g = ev.get(("recT_used", g - 1))
        if vh_g is not None:
            w.wait("scalar", "sT", vh_g)
        w.wait("scalar", "sV", vrec)
        vh = w.inc("scalar", lambda eng: eng.copy(recTMh[:, :], recTM[:, :]),
                   "sS")
        if RECIP_HILO:
            if vh_g is not None:
                w.wait("vector", "sT", vh_g)
            w.wait("vector", "sS", vh)
            vlo_ready = w.inc("vector", lambda eng: eng.tensor_tensor(
                out=recTMl[:, :], in0=recTM[:, :], in1=recTMh[:, :],
                op=ALU.subtract), "sV")
            w.wait("tensor", "sV", vlo_ready)
            srcs = (("h", recTMh), ("l", recTMl))
        else:
            srcs = (("h", recTMh),)

        vnm_oc = {}

        def emit_numer(oc):
            ridxn, slotn, guardn = rotDE.acquire()
            if guardn:
                w.wait("tensor", guardn[0], guardn[1])
            psn = ps_de(slotn)

            def mmn(eng, psn=psn, oc=oc, b=b, s=s):
                return eng.matmul(psn[:, :], kvbd[:, s, oc, :],
                                  qf_sb[:, b * NCH + oc, :],
                                  start=True, stop=True)
            vnm_oc[oc] = (ridxn, psn, w.inc("tensor", mmn, "sT"))

        pre = [rotDE.acquire() for _ in srcs]
        emit_numer(0)
        emit_numer(1)

        w.wait("tensor", "sS", vh)
        rslots = []
        for (which, src), (ridx, slot, guard) in zip(srcs, pre):
            if guard:
                w.wait("tensor", guard[0], guard[1])
            ps = ps_de(slot)
            for sub in range(NSUB):
                def mm(eng, ps=ps, sub=sub, src=src):
                    return eng.matmul(
                        ps[0:16, sub * 128:(sub + 1) * 128],
                        src[:, sub * 16:(sub + 1) * 16], ident[:, :],
                        start=True, stop=True)
                if sub == NSUB - 1:
                    vt = w.inc("tensor", mm, "sT")
                else:
                    w.em("tensor", mm)
            rslots.append((ridx, ps, vt, which))
        ev[("recT_used", g)] = w.c["sT"]
        for ridx, ps, vt, which in rslots:
            w.wait("scalar", "sT", vt)
            dst = recT_h if which == "h" else recT_l

            def evac(eng, ps=ps, dst=dst):
                return eng.copy(dst[:, :], ps[0:16, :])
            vse = w.inc("scalar", evac, "sS")
            rotDE.release(ridx, "sS", vse)
        vre = vse

        w.wait("tensor", "sS", vre)
        for oc in range(NCH):
            ridxb, slotb, guardb = rotAB.acquire()
            if guardb:
                w.wait("tensor", guardb[0], guardb[1])
            psb = ps_ab(slotb)

            def mmh(eng, psb=psb, oc=oc):
                return eng.matmul(psb[:, :], g16[:, oc, :], recT_h[:, :],
                                  start=True, stop=not RECIP_HILO)

            def mml(eng, psb=psb, oc=oc):
                return eng.matmul(psb[:, :], g16[:, oc, :], recT_l[:, :],
                                  start=False, stop=True)
            if RECIP_HILO:
                w.em("tensor", mmh)
                vbc = w.inc("tensor", mml, "sT")
            else:
                vbc = w.inc("tensor", mmh, "sT")

            ridxn, psn, vnm = vnm_oc[oc]

            w.wait("scalar", "sT", vbc)
            if ("attn_evac", g, oc - 2) in ev:
                w.wait("scalar", "sV", ev[("attn_evac", g, oc - 2)])
            elif ("ksq", g - 2, oc + 6) in ev:
                w.wait("scalar", "sP" if KSQ_GP else "sV",
                       ev[("ksq", g - 2, oc + 6)])

            def bevac(eng, psb=psb, oc=oc):
                return eng.copy(bc_sb[:, oc % 2, :], psb[:, :])
            vbe = w.inc("scalar", bevac, "sS")
            rotAB.release(ridxb, "sS", vbe)

            w.wait("vector", "sT", vnm)
            w.wait("vector", "sS", vbe)
            if oc == 0 and ("kvmm_done", g - 2) in ev:
                w.wait("vector", "sT", ev[("kvmm_done", g - 2)])

            def aevac(eng, psn=psn, oc=oc):
                return eng.tensor_tensor(out=attn[:, oc, :], in0=psn[:, :],
                                         in1=bc_sb[:, oc % 2, :],
                                         op=ALU.mult)
            vae = w.inc("vector", aevac, "sV")
            ev[("attn_evac", g, oc)] = vae
            rotDE.release(ridxn, "sV", vae)
            if oc + 2 < NCH:
                emit_numer(oc + 2)
        ev[("numer_done", g)] = vnm_oc[NCH - 1][2]

        if g == 0:
            w.wait("tensor", "sS", ev[("wprep", 3)])
        w.wait("tensor", "sV", ev[("attn_evac", g, NCH - 1)])
        for sub in range(NSUB):
            for half in range(2):
                if guardC[half]:
                    w.wait("tensor", guardC[half][0], guardC[half][1])
                hs = slice(half * 512, (half + 1) * 512)
                for dc in range(NCH):
                    def mm(eng, hs=hs, dc=dc, sub=sub):
                        return eng.matmul(
                            psC[:, hs],
                            attn[:, dc, sub * 128:(sub + 1) * 128],
                            arena[:, 3, dc, hs],
                            start=(dc == 0),
                            stop=(dc == NCH - 1))
                    if dc == NCH - 1:
                        vt = w.inc("tensor", mm, "sT")
                    else:
                        w.em("tensor", mm)
                w.wait("vector", "sT", vt)
                if half == 0:
                    w.wait("vector", "sB", ev[("boacc", g, sub)])

                def x2evac(eng, hs=hs, b=b, sub=sub):
                    return eng.tensor_tensor(
                        out=xin[:, b, sub, hs], in0=psC[:, hs],
                        in1=xin[:, b, sub, hs], op=ALU.add)
                vx2 = w.inc("vector", x2evac, "sV")
                guardC[half] = ("sV", vx2)
                ev[("x2s", g, sub)] = vx2
        ev[("x2", g)] = vx2
        ev[("wo_done", g)] = w.c["sT"]

        ag = None
        if g == 0:
            ag = ("sS", ev[("wprep", 7)])
        elif g == 1:
            ag = ("sT", ev[("tr_mm", "lnA", ngroups - 1)])
        elif ("mix", g - 2) in ev:
            ag = ("sV", ev[("mix", g - 2)])
        emit_ln(g, "lnB", extra_stat_wait=("sV", ev[("x2", g)]),
                apply_guard=ag)

    def b_tail(g):
        s, gi = divmod(g, n_grp)
        b = g % 2
        if gi == 0:
            if ("col0", g - 1) in ev:
                w.wait("gpsimd", "sS", ev[("col0", g - 1)])
            vbnd = w.inc("gpsimd", lambda eng: eng.memset(bnd[:, :], 0.0),
                         "sP")
        else:
            w.wait("gpsimd", "sS", ev[("tr2_evac", g - 1)])
            w.wait("gpsimd", "sV", ev[("tr2_evac_v", g - 1)])
            if ("col0", g - 1) in ev:
                w.wait("gpsimd", "sS", ev[("col0", g - 1)])
            vbnd = w.inc("gpsimd", lambda eng: eng.tensor_copy(
                bnd[:, :], ln2T[:, :, GT:GT + 1]), "sP")

        def ln2_evac(eng, ps, cc, ve=False):
            if ve:
                return eng.tensor_copy(ln2T[:, cc, 1:GT + 1], ps[:, :])
            return eng.copy(ln2T[:, cc, 1:GT + 1], ps[:, :])
        few = [("sP", vbnd)]
        if ("mix", g - 1) in ev:
            few.append(("sV", ev[("mix", g - 1)]))
        vtr2_s, vtr2_v = emit_transpose(g, 0, ln2_evac, "lnB",
                                        first_evac_waits=few)
        ev[("tr2_evac", g)] = vtr2_s
        ev[("tr2_evac_v", g)] = vtr2_v
        w.wait("scalar", "sP", vbnd)
        w.wait("scalar", "sV", vtr2_v)
        vcol0 = w.inc("scalar", lambda eng: eng.copy(
            ln2T[:, :, 0:1], bnd[:, :]), "sS")
        ev[("col0", g)] = vcol0

        w.wait("vector", "sS", vtr2_s)
        w.wait("vector", "sS", vcol0)
        w.wait("vector", "sT", ev[("wo_done", g)])
        w.wait("vector", "sT", ev[("tr_mm", "lnB", g)])
        xkt = xkb(g)
        dview = dview_of(g)
        w.em("vector", lambda eng, dview=dview: eng.tensor_tensor(
            out=dview[:, :, :], in0=ln2T[:, :, 1:GT + 1],
            in1=ln2T[:, :, 0:GT], op=ALU.subtract))
        for cc in range(NCH):
            def mr(eng, cc=cc, b=b, dview=dview):
                return eng.scalar_tensor_tensor(
                    out=qf_sb[:, b * NCH + cc, :], in0=dview[:, cc, :],
                    scalar=mur[:, cc:cc + 1], in1=ln2T[:, cc, 0:GT],
                    op0=ALU.mult, op1=ALU.add)

            def mk(eng, cc=cc, xkt=xkt, dview=dview):
                return eng.scalar_tensor_tensor(
                    out=xkt[:, cc, :], in0=dview[:, cc, :],
                    scalar=muk[:, cc:cc + 1], in1=ln2T[:, cc, 0:GT],
                    op0=ALU.mult, op1=ALU.add)
            w.em("vector", mr)
            if cc == NCH - 1:
                vmix = w.inc("vector", mk, "sV")
            else:
                w.em("vector", mk)
        ev[("mix", g)] = vmix
        if g >= 1:
            w.em("scalar", lambda eng: eng.activation(
                h05[:, :], eps5[:, :], AF.Sigmoid))

    def b_ffn_k(g):
        s, gi = divmod(g, n_grp)
        b = g % 2
        xkt = xkb(g)
        g_att = g + 1 if g + 1 < ngroups else g

        if g == 0:
            w.wait("tensor", "sS", ev[("wprep", 5)])
        w.wait("tensor", "sV", ev[("mix", g)])
        for oc in range(NCH):
            ridx, slot, guard = rotAB.acquire()
            if guard:
                w.wait("tensor", guard[0], guard[1])
            ps = ps_ab(slot)
            for cc in range(NCH):
                def mm(eng, ps=ps, oc=oc, cc=cc, xkt=xkt):
                    return eng.matmul(
                        ps[:, :], arena[:, 0, cc, oc * 128:(oc + 1) * 128],
                        xkt[:, cc, :],
                        start=(cc == 0), stop=(cc == NCH - 1))
                if cc == NCH - 1:
                    vt = w.inc("tensor", mm, "sT")
                else:
                    w.em("tensor", mm)
            w.wait("scalar", "sT", vt)
            if oc < 2:
                w.wait("scalar", "sV", ev[("attn_evac", g_att, oc + 6)])
            else:
                w.wait("scalar", "sP" if KSQ_GP else "sV",
                       ev[("ksq", g, oc - 2)])

            def relu(eng, ps=ps, oc=oc):
                return eng.activation(bc_sb[:, oc % 2, :], ps[:, :], AF.Relu)
            vrl = w.inc("scalar", relu, "sS")
            rotAB.release(ridx, "sS", vrl)
            ksq_e = "gpsimd" if KSQ_GP else "vector"
            ksq_s = "sP" if KSQ_GP else "sV"
            w.wait(ksq_e, "sS", vrl)
            if oc == 0:
                w.wait(ksq_e, "sT", ev[("wo_done", g_att)])

            def sq(eng, oc=oc):
                return eng.tensor_tensor(out=attn[:, oc, :],
                                         in0=bc_sb[:, oc % 2, :],
                                         in1=bc_sb[:, oc % 2, :],
                                         op=ALU.mult)
            ev[("ksq", g, oc)] = w.inc(ksq_e, sq, ksq_s)

    def b_ffn_rest(g):
        s, gi = divmod(g, n_grp)
        b = g % 2

        if g == 0:
            w.wait("tensor", "sS", ev[("wprep", 7)])
        if ("tmul_last", g - 1) in ev:
            w.wait("scalar", "sV", ev[("tmul_last", g - 1)])
        for sub in range(NSUB):
            for half in range(2):
                if guardC[half]:
                    w.wait("tensor", guardC[half][0], guardC[half][1])
                hs = slice(half * 512, (half + 1) * 512)
                for cc in range(NCH):
                    def mm(eng, hs=hs, cc=cc, sub=sub, b=b):
                        return eng.matmul(
                            psC[:, hs],
                            qf_sb[:, b * NCH + cc,
                                  sub * 128:(sub + 1) * 128],
                            arena[:, 2, cc, hs],
                            start=(cc == 0), stop=(cc == NCH - 1))
                    if cc == NCH - 1:
                        vt = w.inc("tensor", mm, "sT")
                    else:
                        w.em("tensor", mm)
                w.wait("scalar", "sT", vt)

                def sig(eng, hs=hs, sub=sub):
                    return eng.activation(r_sb[:, sub, hs], psC[:, hs],
                                          AF.Sigmoid)
                vsg = w.inc("scalar", sig, "sS")
                guardC[half] = ("sS", vsg)
        ev[("r_done", g)] = w.c["sT"]
        if g + 2 < ngroups:
            b_qfload(g + 2)

        if g == 0:
            wp6 = ev[("wprep", 6)]
            if isinstance(wp6, tuple):
                w.wait("tensor", wp6[0], wp6[1])
            else:
                w.wait("tensor", "sS", wp6)
        w.wait("tensor", "sP" if KSQ_GP else "sV",
               ev[("ksq", g, NCH - 1)])
        vtm_last = None
        for sub in range(NSUB):
            for half in range(2):
                if guardC[half]:
                    w.wait("tensor", guardC[half][0], guardC[half][1])
                hs = slice(half * 512, (half + 1) * 512)
                for cc in range(NCH):
                    def mm(eng, hs=hs, cc=cc, sub=sub):
                        return eng.matmul(
                            psC[:, hs],
                            attn[:, cc, sub * 128:(sub + 1) * 128],
                            arena[:, 1, cc, hs],
                            start=(cc == 0), stop=(cc == NCH - 1))
                    if cc == NCH - 1:
                        vt = w.inc("tensor", mm, "sT")
                    else:
                        w.em("tensor", mm)
                w.wait("vector", "sT", vt)

                def tmul(eng, hs=hs, sub=sub):
                    return eng.tensor_tensor(
                        out=r_sb[:, sub, hs], in0=psC[:, hs],
                        in1=r_sb[:, sub, hs], op=ALU.mult)
                vtm = w.inc("vector", tmul, "sV")
                guardC[half] = ("sV", vtm)
                vtm_last = vtm

            def fin(eng, sub=sub, b=b):
                return eng.tensor_tensor(
                    out=xin[:, b, sub, :], in0=r_sb[:, sub, :],
                    in1=xin[:, b, sub, :], op=ALU.add)
            vfin = w.inc("vector", fin, "sV")
            w.wait("gpsimd", "sV", vfin)

            def st(eng, s=s, gi=gi, sub=sub, b=b):
                return eng.dma_start(
                    out_d[s, gi * GT + sub * 128:gi * GT + (sub + 1) * 128,
                          :], xin[:, b, sub, :])
            w.dma("gpsimd", st, "sO")
        ev[("kvmm_done", g)] = w.c["sT"]
        ev[("tmul_last", g)] = vtm_last
        w.em("scalar", lambda eng: eng.activation(
            h05[:, :], eps5[:, :], AF.Sqrt))
        if g + 2 < ngroups:
            b_xload(g + 2)

    attn_scr = attn[:, :, :].rearrange("p a b -> p (a b)")
    prep_weight(0, 0, wds["wq"], True, 0, scratch=attn_scr)
    prep_weight(1, 1, wds["wk"], True, 1)
    prep_weight(2, 2, wds["wv"], False, 0,
                dma_waits=[ev[("wprep_cast", 0, 1)]],
                scratch=attn_scr)
    prep_weight(3, 3, wds["wo"], False, 1,
                dma_waits=[ev[("wprep_cast", 1, 1)]])

    a_xload(0)
    a_xload(1)
    emit_ln_a(0)
    if ngroups > 1:
        emit_ln_a(1)
    for g in range(ngroups):
        phase_a_group(g)

    prep_weight(5, 0, wds["wk2"], False, 1,
                dma_waits=[("sV", ev[("lnA", ngroups - 1)])],
                cast_waits=[("sT", ev[("tr_mm", "lnA", ngroups - 2)])])
    prep_weight(6, 1, wds["wv2"], False, 0,
                dma_waits=[("sV", ev[("lnA", ngroups - 2)])])
    prep_weight(7, 2, wds["wr"], False, 1,
                dma_waits=[ev[("wprep_cast", 5, 1)]])

    b_xload(0)
    b_qfload(0)
    b_xload(1)
    b_qfload(1)
    for g in range(ngroups):
        b_att(g)
        if g >= 1:
            b_ffn_k(g - 1)
        b_tail(g)
        if g >= 1:
            b_ffn_rest(g - 1)
    b_ffn_k(ngroups - 1)
    b_ffn_rest(ngroups - 1)

    w.wait("gpsimd", "sO", 16 * NSUB * ngroups)

    if debug:
        dbg_names = ["stats", "ln_tm", "lnT", "kf_tm", "v_tm", "kvbd",
                     "ksumb", "qf_sb", "recTM", "recT_h", "recT_l", "attn",
                     "xk_sb", "r_sb", "ln2T", "t_sb", "bc_sb", "arena",
                     "cqe", "ckrow", "bvrow", "bo_hi", "bo_lo", "pbf",
                     "ident", "g16", "mur_s", "muk_s"]
        dbg_map = {"stats": stats, "ln_tm": ln_tm, "lnT": lnT,
                   "kf_tm": kf_tm, "v_tm": v_tm, "kvbd": kvbd,
                   "ksumb": ksumb, "qf_sb": qf_sb, "recTM": recTM,
                   "recT_h": recT_h, "recT_l": recT_l, "attn": attn,
                   "xk_sb": xk_sb, "r_sb": r_sb, "ln2T": ln2T,
                   "t_sb": t_sb, "bc_sb": bc_sb, "arena": arena,
                   "cqe": cqe, "ckrow": ckrow, "bvrow": bvrow,
                   "bo_hi": bo_hi, "bo_lo": bo_lo, "pbf": pbf,
                   "ident": ident, "g16": g16, "mur_s": mur, "muk_s": muk}
        for nm in dbg_names:
            t = dbg_map[nm]
            shp = list(t.shape) if hasattr(t, "shape") else None
            d = nc.dram_tensor("dbg_" + nm, shp, t.dtype,
                               kind="ExternalOutput")
            full = t[tuple(slice(None) for _ in shp)]
            w.dma("gpsimd", lambda eng, d=d, full=full, shp=shp:
                  eng.dma_start(
                      d[tuple(slice(None) for _ in shp)], full), "sO")
        w.wait("gpsimd", "sO", w.c["sO"])

    with nc.Block() as blk:

        @blk.sync
        def _(eng):
            for f in w.q["sync"]:
                f(eng)

        @blk.tensor
        def _(eng):
            for f in w.q["tensor"]:
                f(eng)

        @blk.vector
        def _(eng):
            for f in w.q["vector"]:
                f(eng)

        @blk.scalar
        def _(eng):
            for f in w.q["scalar"]:
                f(eng)

        @blk.gpsimd
        def _(eng):
            for f in w.q["gpsimd"]:
                f(eng)

    return nc


def _gsel_const():
    g = np.zeros((16, NCH, 128), np.float32)
    for oc in range(NCH):
        g[2 * oc, oc, 0:64] = 1.0
        g[2 * oc + 1, oc, 64:128] = 1.0
    return g


def kernel(L0_attn_x, L0_attn_Wq, L0_attn_bq, L0_attn_Wk, L0_attn_bk,
           L0_attn_Wv, L0_attn_bv, L0_attn_Wo, L0_attn_bo, L0_attn_P,
           L0_ffn_W_r, L0_ffn_W_k, L0_ffn_W_v, L0_ffn_mu_r, L0_ffn_mu_k):
    x = np.ascontiguousarray(np.asarray(L0_attn_x, dtype=np.float32))
    B = x.shape[0]
    n_cores = 8
    per = B // n_cores
    nc = build_program(n_seq=per, n_grp=x.shape[1] // GT)
    common = dict(
        wq=np.ascontiguousarray(np.asarray(L0_attn_Wq, np.float32)),
        wk=np.ascontiguousarray(np.asarray(L0_attn_Wk, np.float32)),
        wv=np.ascontiguousarray(np.asarray(L0_attn_Wv, np.float32)),
        wo=np.ascontiguousarray(np.asarray(L0_attn_Wo, np.float32)),
        wr=np.ascontiguousarray(np.asarray(L0_ffn_W_r, np.float32)),
        wk2=np.ascontiguousarray(np.asarray(L0_ffn_W_k, np.float32)),
        wv2=np.ascontiguousarray(np.asarray(L0_ffn_W_v, np.float32)),
        bq=np.ascontiguousarray(np.asarray(L0_attn_bq, np.float32)),
        bk=np.ascontiguousarray(np.asarray(L0_attn_bk, np.float32)),
        bv=np.ascontiguousarray(np.asarray(L0_attn_bv, np.float32)),
        bo=np.ascontiguousarray(np.asarray(L0_attn_bo, np.float32)),
        p=np.ascontiguousarray(np.asarray(L0_attn_P, np.float32)),
        mur=np.ascontiguousarray(np.asarray(L0_ffn_mu_r, np.float32)),
        muk=np.ascontiguousarray(np.asarray(L0_ffn_mu_k, np.float32)),
        gsel=_gsel_const(),
    )
    in_maps = []
    for c in range(n_cores):
        m = dict(common)
        m["x"] = np.ascontiguousarray(x[c * per:(c + 1) * per])
        in_maps.append(m)
    res = run_bass_kernel_spmd(nc, in_maps, core_ids=list(range(n_cores)))
    return np.concatenate([r["out"] for r in res.results], axis=0)


# version 28
# speedup vs baseline: 1.1212x; 1.1212x over previous
import sys

sys.path.insert(0, "/opt/trn_rl_repo")

import numpy as np
import concourse.bass as bass
import concourse.mybir as mybir
from concourse.bass_utils import run_bass_kernel_spmd

F32 = mybir.dt.float32
BF16 = mybir.dt.bfloat16
I32 = mybir.dt.int32
AF = mybir.ActivationFunctionType
ALU = mybir.AluOpType
AX = mybir.AxisListType

D = 1024
NCH = 8
NHEAD = 16
NORM = 1.0 / (8.0 ** 0.5)
EPS_F = 1e-6
EPS_D = 1e-6
EPS_LN = 1e-5
GT = 512
NSUB = 4
RECIP_HILO = True
KSQ_GP = True
TR_ALT = True
HOIST_Q = True

ENGS = ("sync", "tensor", "vector", "scalar", "gpsimd")


class W:
    def __init__(self, sems):
        self.q = {e: [] for e in ENGS}
        self.c = {name: 0 for name in sems}
        self.sems = sems

    def em(self, e, f):
        self.q[e].append(f)

    def inc(self, e, f, sem, n=1):
        self.c[sem] += n
        s = self.sems[sem]

        def g(eng, f=f, s=s, n=n):
            f(eng).then_inc(s, n)

        self.q[e].append(g)
        return self.c[sem]

    def wait(self, e, sem, val):
        if val is None or val <= 0:
            return
        s = self.sems[sem]
        self.q[e].append(lambda eng, s=s, v=val: eng.wait_ge(s, v))

    def dma(self, e, f, sem):
        self.wait(e, sem, self.c[sem])
        return self.inc(e, f, sem, 16)


class Rot:

    def __init__(self, depth=2):
        self.hist = []
        self.depth = depth

    def acquire(self):
        i = len(self.hist)
        d = self.depth
        self.hist.append(None)
        guard = self.hist[i - d] if i >= d else None
        if i >= d:
            assert guard is not None, "psum slot acquired before prior release"
        return i, i % d, guard

    def release(self, idx, sem, val):
        self.hist[idx] = (sem, val)


def build_program(n_seq=4, n_grp=4, debug=False):
    seq_len = n_grp * GT
    ngroups = n_seq * n_grp

    nc = bass.Bass("TRN2", target_bir_lowering=False,
                   detect_race_conditions=False)

    x_d = nc.dram_tensor("x", [n_seq, seq_len, D], F32, kind="ExternalInput")
    wds = {}
    for nme in ("wq", "wk", "wv", "wo", "wr", "wk2", "wv2"):
        wds[nme] = nc.dram_tensor(nme, [D, D], F32, kind="ExternalInput")
    bq_d = nc.dram_tensor("bq", [D], F32, kind="ExternalInput")
    bk_d = nc.dram_tensor("bk", [D], F32, kind="ExternalInput")
    bv_d = nc.dram_tensor("bv", [D], F32, kind="ExternalInput")
    bo_d = nc.dram_tensor("bo", [D], F32, kind="ExternalInput")
    p_d = nc.dram_tensor("p", [64, 64], F32, kind="ExternalInput")
    mur_d = nc.dram_tensor("mur", [D], F32, kind="ExternalInput")
    gsel_d = nc.dram_tensor("gsel", [16, NCH, 128], F32, kind="ExternalInput")
    muk_d = nc.dram_tensor("muk", [D], F32, kind="ExternalInput")

    out_d = nc.dram_tensor("out", [n_seq, seq_len, D], F32,
                           kind="ExternalOutput")
    qfs_d = nc.dram_tensor("qfs", [ngroups, 128, NCH, GT], BF16,
                           kind="ExternalOutput" if debug else "Internal")

    al = nc.alloc_sbuf_tensor
    arena = al("arena", [128, 4, NCH, D], BF16)
    xin = al("xin", [128, 2, NSUB, D], F32)
    ln_tm = al("ln_tm", [128, 2, NSUB, D], BF16)
    lnT = al("lnT", [128, 2 * NCH, GT], BF16)
    qf_sb = al("qf_sb", [128, 2 * NCH, GT], BF16)
    kf_tm = al("kf_tm", [128, D], BF16)
    v_tm = al("v_tm", [128, D], BF16)
    kvbd = al("kvbd", [128, n_seq, NCH, 128], BF16)
    ksumb = al("ksumb", [128, n_seq, NCH, 16], BF16)
    attn = al("attn", [128, NCH, GT], BF16)
    xk_sb = al("xk_sb", [128, NCH, GT], BF16)
    ln2T = al("ln2T", [128, NCH, GT + 1], BF16)
    bnd = al("bnd", [128, NCH], BF16)
    r_sb = al("r_sb", [128, NSUB, D], BF16)
    stats = al("stats", [128, 2, NSUB, 8], F32)
    sqacc = al("sqacc", [128, 2, NSUB], F32)
    recTMe = al("recTMe", [128, NSUB * 16], F32)
    recTM = al("recTM", [128, NSUB * 16], F32)
    recTMh = al("recTMh", [128, NSUB * 16], BF16)
    recTMl = al("recTMl", [128, NSUB * 16], BF16)
    recT_h = al("recT_h", [16, GT], BF16)
    recT_l = al("recT_l", [16, GT], BF16)
    bc_sb = al("bc_sb", [128, 2, GT], F32)

    idx = al("idx", [128, 128], I32)
    ident = al("ident", [128, 128], BF16)
    identf = al("identf", [128, 128], F32)
    g16 = al("g16", [16, NCH, 128], BF16)
    ones_b = al("ones_b", [1, 128], BF16)
    ones_c = al("ones_c", [128, 1], BF16)
    ones4 = al("ones4", [128, NSUB], F32)
    zb = al("zb", [128, 128], BF16)
    eps5 = al("eps5", [128, 1], F32)
    h05 = al("h05", [128, 1], F32)
    mhalf = al("mhalf", [128, 1], F32)
    mone = al("mone", [128, 1], F32)
    bv_bc = al("bv_bc", [128, D], BF16)
    pf = al("pf", [64, 64], F32)
    pbf = al("pbf", [128, 128], F32)
    pbb = al("pbb", [128, 128], BF16)
    cqe = al("cqe", [128, NCH], F32)
    ckrow = al("ckrow", [1, D], BF16)
    bvrow = al("bvrow", [1, D], BF16)
    bcol = al("bcol", [128, NCH], F32)
    mur = al("mur_s", [128, NCH], F32)
    muk = al("muk_s", [128, NCH], F32)

    ap = nc.alloc_psum_tensor
    psA = ap("psA", [128, 512], F32)
    psB = ap("psB", [128, 512], F32)
    psC = ap("psC", [128, 1024], F32)
    psD = ap("psD", [128, 512], F32)
    psE = ap("psE", [128, 512], F32)
    psKV = ap("psKV", [128, 512], F32)
    psKS = ap("psKS", [128, 128], F32)

    sems = {n: nc.alloc_semaphore(n) for n in
            ("sX", "sW", "sQ", "sG", "sO", "sT", "sV", "sS", "sP", "sB")}
    w = W(sems)
    ev = {}
    rotDE = Rot(4)
    rotAB = rotDE
    guardC = {0: None, 1: None}
    _ps_pool = None

    def ps_de(slot):
        return (psD, psE, psA, psB)[slot]

    def ps_ab(slot):
        return (psD, psE, psA, psB)[slot]

    with nc.Block() as blk:

        @blk.sync
        def _(eng):
            eng.dma_start(pf[:, :], p_d[:, :]).then_inc(sems["sX"], 16)
            eng.dma_start(
                bcol[:, :], bq_d[:].rearrange("(j p) -> p j", p=128),
                allow_slow_non_contiguous=True).then_inc(sems["sX"], 16)
            eng.dma_start(
                mur[:, :], mur_d[:].rearrange("(j p) -> p j", p=128),
                allow_slow_non_contiguous=True).then_inc(sems["sX"], 16)
            eng.dma_start(
                muk[:, :], muk_d[:].rearrange("(j p) -> p j", p=128),
                allow_slow_non_contiguous=True).then_inc(sems["sX"], 16)
            eng.wait_ge(sems["sX"], 16 * 4)

        @blk.gpsimd
        def _(eng):
            eng.iota(idx[:, :], [[1, 128]], channel_multiplier=-1)
            eng.memset(ones_b[:, :], 1.0)
            eng.memset(ones_c[:, :], 1.0)
            eng.memset(ones4[:, :], 1.0)
            eng.memset(zb[:, :], 0.0)
            eng.memset(eps5[:, :], EPS_LN)
            eng.memset(h05[:, :], 0.5)
            eng.memset(mhalf[:, :], -0.5)
            eng.memset(mone[:, :], -1.0)
            eng.memset(pbf[:, :], 0.0)
            eng.memset(kvbd[:, :, :, :], 0.0)
            eng.memset(ksumb[:, :, :, :], 0.0)
            eng.memset(stats[:, :, :, :], 0.0)
            eng.dma_start(g16[:, :, :], gsel_d[:, :, :]).then_inc(
                sems["sB"], 16)
            eng.wait_ge(sems["sB"], 16)

    with nc.Block() as blk:

        @blk.vector
        def _(eng):
            eng.tensor_scalar(out=ident[:, :], in0=idx[:, :], scalar1=0.0,
                              scalar2=None, op0=ALU.is_equal)
            eng.tensor_scalar(out=identf[:, :], in0=idx[:, :], scalar1=0.0,
                              scalar2=None, op0=ALU.is_equal)

    with nc.Block() as blk:

        @blk.tensor
        def _(eng):
            eng.matmul(psA[0:64, 0:64], pf[:, :], identf[0:64, 0:64],
                       start=True, stop=True)

    with nc.Block() as blk:

        @blk.vector
        def _(eng):
            eng.tensor_scalar(out=pbf[0:64, 0:64], in0=psA[0:64, 0:64],
                              scalar1=NORM, scalar2=None, op0=ALU.mult)
            eng.tensor_scalar(out=pbf[64:128, 64:128], in0=psA[0:64, 0:64],
                              scalar1=NORM, scalar2=None, op0=ALU.mult)

    with nc.Block() as blk:

        @blk.scalar
        def _(eng):
            eng.copy(pbb[:, :], pbf[:, :])

    with nc.Block() as blk:

        @blk.tensor
        def _(eng):
            for dc in range(NCH):
                eng.matmul(psKS[:, dc:dc + 1], pbf[:, :], bcol[:, dc:dc + 1],
                           start=True, stop=True)

    with nc.Block() as blk:

        @blk.vector
        def _(eng):
            eng.tensor_scalar(out=cqe[:, :], in0=psKS[:, 0:NCH],
                              scalar1=EPS_F, scalar2=None, op0=ALU.add)

        @blk.sync
        def _(eng):
            eng.dma_start(
                bcol[:, :], bk_d[:].rearrange("(j p) -> p j", p=128),
                allow_slow_non_contiguous=True).then_inc(sems["sX"], 16)
            eng.wait_ge(sems["sX"], 16 * 5)

    with nc.Block() as blk:

        @blk.tensor
        def _(eng):
            for dc in range(NCH):
                eng.matmul(psC[0:1, dc * 128:(dc + 1) * 128],
                           bcol[:, dc:dc + 1], pbf[:, :],
                           start=True, stop=True)

    with nc.Block() as blk:

        @blk.scalar
        def _(eng):
            eng.copy(ckrow[:, :], psC[0:1, :])

        @blk.gpsimd
        def _(eng):
            eng.dma_start(bvrow[:, :], bv_d[:].unsqueeze(0)).then_inc(
                sems["sB"], 16)
            eng.wait_ge(sems["sB"], 32)

    with nc.Block() as blk:

        @blk.tensor
        def _(eng):
            for half in range(2):
                eng.matmul(psC[:, half * 512:(half + 1) * 512],
                           ones_b[:, :], bvrow[:, half * 512:(half + 1) * 512],
                           start=True, stop=True)

    with nc.Block() as blk:

        @blk.scalar
        def _(eng):
            eng.copy(bv_bc[:, :], psC[:, :])

    w.c["sX"] = 16 * 5

    def prep_weight(eid, arena_slot, wd, fuse_p, buf,
                    dma_waits=(), cast_waits=(), scratch=None,
                    evac_scale=None):
        if scratch is None:
            scratch = ln_tm[:, 0, :, :].rearrange("p a b -> p (a b)")
        skey = ("prep_trmm", scratch.tensor.name)
        sview = scratch.rearrange("p (a b) -> p a b", a=NSUB)
        dsem = "sX" if eid % 2 == 0 else "sW"
        ceng = "vector" if scratch.tensor.name == "attn" else "scalar"
        csem = "sV" if ceng == "vector" else "sS"
        for half in range(2):
            def ld(eng, wd=wd, half=half, buf=buf):
                return eng.dma_start(
                    xin[:, buf, :, :],
                    wd[:, :].rearrange("(dc p) c -> p dc c", p=128)[
                        :, 4 * half:4 * half + 4, :])
            if half == 0:
                for (e_sem, e_val) in dma_waits:
                    w.wait("sync", e_sem, e_val)
            else:
                w.wait("sync", csem, vc)
            vx = w.dma("sync", ld, dsem)
            w.wait(ceng, dsem, vx)
            if half == 0:
                for (e_sem, e_val) in cast_waits:
                    w.wait(ceng, e_sem, e_val)
            if ev.get(skey):
                w.wait(ceng, "sT", ev[skey])
            if ceng == "vector":
                vc = w.inc(ceng, lambda eng, buf=buf, scratch=scratch:
                           eng.tensor_copy(scratch,
                                           xin[:, buf, :, :].rearrange(
                                               "p a b -> p (a b)")), csem)
            else:
                vc = w.inc(ceng, lambda eng, buf=buf, scratch=scratch:
                           eng.copy(scratch,
                                    xin[:, buf, :, :].rearrange(
                                        "p a b -> p (a b)")), csem)
            ev[("wprep_cast", eid, half)] = (csem, vc)
            w.wait("tensor", csem, vc)
            for cc in range(NCH):
                ridx, slot, guard = rotDE.acquire()
                if guard:
                    w.wait("tensor", guard[0], guard[1])
                ps = ps_de(slot)
                for dc in range(4):
                    def mm(eng, ps=ps, dc=dc, cc=cc, fuse_p=fuse_p,
                           sview=sview):
                        return eng.matmul(
                            ps[:, dc * 128:(dc + 1) * 128],
                            sview[:, dc, cc * 128:(cc + 1) * 128],
                            pbb[:, :] if fuse_p else ident[:, :],
                            start=True, stop=True)
                    if dc == 3:
                        vt = w.inc("tensor", mm, "sT")
                    else:
                        w.em("tensor", mm)
                ee = "scalar" if (cc % 2 == 0 and evac_scale is None) \
                    else "vector"
                esem = "sS" if ee == "scalar" else "sV"
                w.wait(ee, "sT", vt)
                if evac_scale is not None:
                    def evac(eng, ps=ps, slot_=arena_slot, cc=cc, half=half,
                             sc_=evac_scale):
                        return eng.tensor_scalar(
                            out=arena[:, slot_, cc,
                                      half * 512:(half + 1) * 512],
                            in0=ps[:, :], scalar1=sc_, scalar2=None,
                            op0=ALU.mult)
                elif ee == "scalar":
                    def evac(eng, ps=ps, slot_=arena_slot, cc=cc, half=half):
                        return eng.copy(
                            arena[:, slot_, cc,
                                  half * 512:(half + 1) * 512], ps[:, :])
                else:
                    def evac(eng, ps=ps, slot_=arena_slot, cc=cc, half=half):
                        return eng.tensor_copy(
                            arena[:, slot_, cc,
                                  half * 512:(half + 1) * 512], ps[:, :])
                vse = w.inc(ee, evac, esem)
                rotDE.release(ridx, esem, vse)
                if ee == "scalar":
                    vse_s = vse
            ev[skey] = vt
        if evac_scale is not None:
            ev[("wprep", eid)] = ("sV", vse)
        else:
            ev[("wprep", eid)] = vse_s

    def prep_weight_b(eid, arena_slot, wd, dma_waits_h=((), ()),
                      cast_waits_h=((), ())):
        for half in range(2):
            buf = 1 - half
            scratch = ln_tm[:, half, :, :].rearrange("p a b -> p (a b)")
            sview = scratch.rearrange("p (a b) -> p a b", a=NSUB)
            skey = ("prep_trmm_b", half)
            dsem = "sX" if half == 0 else "sW"
            ceng = "scalar" if half == 0 else "vector"
            csem = "sS" if half == 0 else "sV"

            def ld(eng, wd=wd, half=half, buf=buf):
                return eng.dma_start(
                    xin[:, buf, :, :],
                    wd[:, :].rearrange("(dc p) c -> p dc c", p=128)[
                        :, 4 * half:4 * half + 4, :])
            for (e_sem, e_val) in dma_waits_h[half]:
                w.wait("sync", e_sem, e_val)
            vx = w.dma("sync", ld, dsem)
            w.wait(ceng, dsem, vx)
            for (e_sem, e_val) in cast_waits_h[half]:
                w.wait(ceng, e_sem, e_val)
            if ev.get(skey):
                w.wait(ceng, "sT", ev[skey])
            if ceng == "vector":
                vc = w.inc(ceng, lambda eng, buf=buf, scratch=scratch:
                           eng.tensor_copy(scratch,
                                           xin[:, buf, :, :].rearrange(
                                               "p a b -> p (a b)")), csem)
            else:
                vc = w.inc(ceng, lambda eng, buf=buf, scratch=scratch:
                           eng.copy(scratch,
                                    xin[:, buf, :, :].rearrange(
                                        "p a b -> p (a b)")), csem)
            ev[("wprep_cast", eid, half)] = (csem, vc)
            w.wait("tensor", csem, vc)
            for cc in range(NCH):
                ridx, slot, guard = rotDE.acquire()
                if guard:
                    w.wait("tensor", guard[0], guard[1])
                ps = ps_de(slot)
                for dc in range(4):
                    def mm(eng, ps=ps, dc=dc, cc=cc, sview=sview):
                        return eng.matmul(
                            ps[:, dc * 128:(dc + 1) * 128],
                            sview[:, dc, cc * 128:(cc + 1) * 128],
                            ident[:, :], start=True, stop=True)
                    if dc == 3:
                        vt = w.inc("tensor", mm, "sT")
                    else:
                        w.em("tensor", mm)
                ee = "scalar" if cc % 2 == 0 else "vector"
                esem = "sS" if ee == "scalar" else "sV"
                w.wait(ee, "sT", vt)

                def evac(eng, ps=ps, slot_=arena_slot, cc=cc, half=half,
                         v=(ee == "vector")):
                    if v:
                        return eng.tensor_copy(
                            arena[:, slot_, cc,
                                  half * 512:(half + 1) * 512], ps[:, :])
                    return eng.copy(
                        arena[:, slot_, cc,
                              half * 512:(half + 1) * 512], ps[:, :])
                vse = w.inc(ee, evac, esem)
                rotDE.release(ridx, esem, vse)
                if ee == "scalar":
                    vse_s = vse
                else:
                    vse_v = vse
            ev[skey] = vt
        ev[("wprep", eid)] = vse_s
        ev[("wprep_v", eid)] = vse_v

    def emit_ln(g, apply_key, extra_stat_wait=None, apply_guard=None,
                stat_waits=None):
        b = g % 2
        sb_ = g % 2
        prev = ev.get((apply_key, g - 2))
        if prev is not None:
            w.wait("vector", "sV", prev)
            w.wait("scalar", "sV", prev)
        if extra_stat_wait is not None:
            w.wait("vector", extra_stat_wait[0], extra_stat_wait[1])
            w.wait("scalar", extra_stat_wait[0], extra_stat_wait[1])
        if apply_guard is not None:
            w.wait("scalar", apply_guard[0], apply_guard[1])
        w.em("scalar", lambda eng, sb_=sb_: eng.memzero(sqacc[:, sb_, :]))
        vred = w.inc("vector", lambda eng, sb_=sb_, b=b: eng.reduce_sum(
            stats[:, sb_, :, 0:1], xin[:, b, :, :], axis=AX.X), "sV")
        w.wait("vector", "sV", vred)
        for s_ in range(NSUB):
            vsq = w.inc("scalar", lambda eng, sb_=sb_, b=b, s_=s_: eng.activation(
                ln_tm[:, b, s_, :], xin[:, b, s_, :], AF.Square,
                accum_out=sqacc[:, sb_, s_:s_ + 1]), "sS")
        w.wait("vector", "sS", vsq)
        w.em("vector", lambda eng, sb_=sb_: eng.tensor_scalar(
            out=stats[:, sb_, :, 2:3], in0=stats[:, sb_, :, 0:1],
            scalar1=1.0 / D, scalar2=None, op0=ALU.mult))
        w.em("vector", lambda eng, sb_=sb_: eng.tensor_tensor(
            out=stats[:, sb_, :, 3:4], in0=stats[:, sb_, :, 2:3],
            in1=stats[:, sb_, :, 2:3], op=ALU.mult))
        vvar = w.inc("vector", lambda eng, sb_=sb_: eng.scalar_tensor_tensor(
            out=stats[:, sb_, :, 4:5], in0=sqacc[:, sb_, :].unsqueeze(-1),
            scalar=1.0 / D, in1=stats[:, sb_, :, 3:4],
            op0=ALU.mult, op1=ALU.subtract), "sV")
        w.wait("scalar", "sV", vvar)
        vstd = w.inc("scalar", lambda eng, sb_=sb_: eng.activation(
            stats[:, sb_, :, 5:6], stats[:, sb_, :, 4:5],
            AF.Sqrt, bias=eps5[:, :]), "sS")
        w.wait("vector", "sS", vstd)
        w.em("vector", lambda eng, sb_=sb_: eng.reciprocal(
            stats[:, sb_, :, 6:7], stats[:, sb_, :, 5:6]))
        if apply_guard is not None:
            w.wait("vector", apply_guard[0], apply_guard[1])
        for s_ in range(NSUB):
            vap = w.inc("vector", lambda eng, sb_=sb_, b=b, s_=s_:
                        eng.tensor_scalar(
                            out=ln_tm[:, b, s_, :], in0=xin[:, b, s_, :],
                            scalar1=stats[:, sb_, s_, 2:3],
                            scalar2=stats[:, sb_, s_, 6:7],
                            op0=ALU.subtract, op1=ALU.mult), "sV")
        ev[(apply_key, g)] = vap
        return vap

    def emit_transpose(g, dstT_base, evac_dst_fn, apply_key,
                       first_evac_waits=()):
        b = g % 2
        w.wait("tensor", "sV", ev[(apply_key, g)])
        last = {"scalar": None, "vector": None}
        for cc in range(NCH):
            ridx, slot, guard = rotDE.acquire()
            if guard:
                w.wait("tensor", guard[0], guard[1])
            ps = ps_de(slot)
            for s_ in range(NSUB):
                def mm(eng, ps=ps, b=b, s_=s_, cc=cc):
                    return eng.matmul(
                        ps[:, s_ * 128:(s_ + 1) * 128],
                        ln_tm[:, b, s_, cc * 128:(cc + 1) * 128],
                        ident[:, :], start=True, stop=True)
                if s_ == NSUB - 1:
                    vt = w.inc("tensor", mm, "sT")
                else:
                    w.em("tensor", mm)
            ev[("tr_mm", apply_key, g)] = vt
            e = "scalar" if (cc % 2 == 0 or not TR_ALT) else "vector"
            sem = "sS" if e == "scalar" else "sV"
            w.wait(e, "sT", vt)
            if cc in (0, 1):
                for (e_sem, e_val) in first_evac_waits:
                    w.wait(e, e_sem, e_val)
            vse = w.inc(e, lambda eng, ps=ps, cc=cc, v=(e == "vector"):
                        evac_dst_fn(eng, ps, cc, ve=v), sem)
            rotDE.release(ridx, sem, vse)
            last[e] = vse
        return last["scalar"], last["vector"]

    def a_xload(g):
        s, gi = divmod(g, n_grp)
        if g >= 2:
            w.wait("sync", "sV", ev[("lnA", g - 2)])
        elif g == 0:
            w.wait("sync", *ev[("wprep_cast", 2, 1)])
        elif g == 1:
            w.wait("sync", *ev[("wprep_cast", 3, 1)])

        def ld(eng, s=s, gi=gi, g=g):
            return eng.dma_start(
                xin[:, g % 2, :, :],
                x_d[s, gi * GT:(gi + 1) * GT, :].rearrange(
                    "(n p) c -> p n c", p=128))
        ev[("xloadA", g)] = w.dma("sync", ld, "sX")

    def emit_ln_a(g):
        apply_guard = None
        if g == 0:
            apply_guard = ("sS", ev[("wprep", 3)])
        elif ("tr_mm", "lnA", g - 2) in ev:
            apply_guard = ("sT", ev[("tr_mm", "lnA", g - 2)])
        emit_ln(g, "lnA", extra_stat_wait=("sX", ev[("xloadA", g)]),
                apply_guard=apply_guard)

    def ln_a_stats_pieces(g):
        b = g % 2
        sb_ = g % 2
        prev = ev.get(("lnA", g - 2))
        guard = ev[("tr_mm", "lnA", g - 2)]
        box = {}

        def p0():
            if prev is not None:
                w.wait("vector", "sV", prev)
                w.wait("scalar", "sV", prev)
            w.wait("vector", "sX", ev[("xloadA", g)])
            w.wait("scalar", "sX", ev[("xloadA", g)])
            w.wait("scalar", "sT", guard)
            w.em("scalar", lambda eng: eng.memzero(sqacc[:, sb_, :]))
            box["red"] = w.inc("vector", lambda eng: eng.reduce_sum(
                stats[:, sb_, :, 0:1], xin[:, b, :, :], axis=AX.X), "sV")

        def sq(lo, hi):
            def f():
                for s_ in range(lo, hi):
                    box["sq"] = w.inc(
                        "scalar", lambda eng, s_=s_: eng.activation(
                            ln_tm[:, b, s_, :], xin[:, b, s_, :], AF.Square,
                            accum_out=sqacc[:, sb_, s_:s_ + 1]), "sS")
            return f

        def p3():
            w.wait("vector", "sV", box["red"])
            w.wait("vector", "sS", box["sq"])
            w.em("vector", lambda eng: eng.tensor_scalar(
                out=stats[:, sb_, :, 2:3], in0=stats[:, sb_, :, 0:1],
                scalar1=1.0 / D, scalar2=None, op0=ALU.mult))
            w.em("vector", lambda eng: eng.tensor_tensor(
                out=stats[:, sb_, :, 3:4], in0=stats[:, sb_, :, 2:3],
                in1=stats[:, sb_, :, 2:3], op=ALU.mult))
            vvar = w.inc("vector", lambda eng: eng.scalar_tensor_tensor(
                out=stats[:, sb_, :, 4:5], in0=sqacc[:, sb_, :].unsqueeze(-1),
                scalar=1.0 / D, in1=stats[:, sb_, :, 3:4],
                op0=ALU.mult, op1=ALU.subtract), "sV")
            w.wait("scalar", "sV", vvar)
            vstd = w.inc("scalar", lambda eng: eng.activation(
                stats[:, sb_, :, 5:6], stats[:, sb_, :, 4:5],
                AF.Sqrt, bias=eps5[:, :]), "sS")
            w.wait("vector", "sS", vstd)
            w.em("vector", lambda eng: eng.reciprocal(
                stats[:, sb_, :, 6:7], stats[:, sb_, :, 5:6]))
        return [p0, sq(0, 2), sq(2, 4), p3]

    def ln_a_apply_piece(g, s_):
        b = g % 2
        sb_ = g % 2
        if s_ == 0:
            w.wait("vector", "sT", ev[("tr_mm", "lnA", g - 2)])
        vap = w.inc("vector", lambda eng: eng.tensor_scalar(
            out=ln_tm[:, b, s_, :], in0=xin[:, b, s_, :],
            scalar1=stats[:, sb_, s_, 2:3],
            scalar2=stats[:, sb_, s_, 6:7],
            op0=ALU.subtract, op1=ALU.mult), "sV")
        if s_ == NSUB - 1:
            ev[("lnA", g)] = vap

    def phase_a_group(g):
        s, gi = divmod(g, n_grp)
        b = g % 2

        def lnT_evac(eng, ps, cc, b=b, ve=False):
            if ve:
                return eng.tensor_copy(lnT[:, b * NCH + cc, :], ps[:, :])
            return eng.copy(lnT[:, b * NCH + cc, :], ps[:, :])
        few = []
        if ("kvacc", g - 2, NSUB - 1) in ev:
            few.append(("sT", ev[("kvacc", g - 2, NSUB - 1)]))
        vtr_s, vtr_v = emit_transpose(g, b, lnT_evac, "lnA",
                                      first_evac_waits=few)
        if debug and g == 1:
            dstat = nc.dram_tensor("dbg_stats_g1", [128, NSUB, 8], F32,
                                   kind="ExternalOutput")
            w.wait("gpsimd", "sP", ev[("lnA", 1)])
            w.dma("gpsimd", lambda eng, dstat=dstat: eng.dma_start(
                dstat[:, :, :], stats[:, 1, :, :]), "sO")
            w.wait("gpsimd", "sS", vtr_s)
            if vtr_v:
                w.wait("gpsimd", "sV", vtr_v)
            dsnap = nc.dram_tensor("dbg_lnT_g1", [128, 2 * NCH, GT], BF16,
                                   kind="ExternalOutput")
            w.dma("gpsimd", lambda eng, dsnap=dsnap: eng.dma_start(
                dsnap[:, :, :], lnT[:, :, :]), "sO")
        if g + 2 < ngroups:
            a_xload(g + 2)
        apply_next = (g + 1 < ngroups and ("lnA", g + 1) not in ev)
        w.wait("tensor", "sS", vtr_s)
        w.wait("tensor", "sV", vtr_v)
        if g == 0:
            w.wait("tensor", "sS", ev[("wprep", 0)])
        for oc in range(NCH):
            ridx, slot, guard = rotAB.acquire()
            if guard:
                w.wait("tensor", guard[0], guard[1])
            ps = ps_ab(slot)
            for cc in range(NCH):
                def mm(eng, ps=ps, oc=oc, cc=cc, b=b):
                    return eng.matmul(
                        ps[:, :], arena[:, 0, cc, oc * 128:(oc + 1) * 128],
                        lnT[:, b * NCH + cc, :],
                        start=(cc == 0), stop=(cc == NCH - 1))
                if cc == NCH - 1:
                    vt = w.inc("tensor", mm, "sT")
                else:
                    w.em("tensor", mm)
            w.wait("vector", "sT", vt)
            if g >= 2 and oc == 0:
                w.wait("vector", "sQ", 16 * (g - 1))

            def evac(eng, ps=ps, oc=oc, b=b):
                return eng.tensor_scalar(
                    out=qf_sb[:, b * NCH + oc, :], in0=ps[:, :],
                    scalar1=cqe[:, oc:oc + 1], scalar2=EPS_F,
                    op0=ALU.add, op1=ALU.max)
            vqe = w.inc("vector", evac, "sV")
            rotAB.release(ridx, "sV", vqe)
            if apply_next and 2 <= oc <= 5:
                ln_a_apply_piece(g + 1, oc - 2)
        ev[("qf_evac", g)] = vqe
        w.wait("gpsimd", "sV", vqe)
        w.dma("gpsimd", lambda eng, g=g, b=b: eng.dma_start(
            qfs_d[g, :, :, :], qf_sb[:, b * NCH:(b + 1) * NCH, :]), "sQ")

        ln_pieces = ln_a_stats_pieces(g + 2) if g + 2 < ngroups else None
        if g == 0:
            w.wait("tensor", "sS", ev[("wprep", 1)])
            w.wait("tensor", "sS", ev[("wprep", 2)])
        if gi == 0 and ("kvbd_evac", s - 1) in ev:
            w.wait("tensor", "sS", ev[("kvbd_evac", s - 1)])
        for sub in range(NSUB):
            nglob = gi * NSUB + sub
            kv_ev = {}
            for which, slot_, brow_ in (("k", 1, ckrow), ("v", 2, bvrow)):
                for half in range(2):
                    if guardC[half]:
                        w.wait("tensor", guardC[half][0], guardC[half][1])
                    hs = slice(half * 512, (half + 1) * 512)
                    closed = which == "v"
                    for cc in range(NCH):
                        def mm(eng, hs=hs, cc=cc, b=b, sub=sub, slot_=slot_,
                               closed=closed):
                            return eng.matmul(
                                psC[:, hs],
                                lnT[:, b * NCH + cc,
                                    sub * 128:(sub + 1) * 128],
                                arena[:, slot_, cc, hs],
                                start=(cc == 0),
                                stop=(closed and cc == NCH - 1))
                        if closed and cc == NCH - 1:
                            vt = w.inc("tensor", mm, "sT")
                        else:
                            w.em("tensor", mm)

                    if which == "k":
                        def mmb(eng, hs=hs, brow_=brow_):
                            return eng.matmul(
                                psC[:, hs], ones_b[:, :], brow_[:, hs],
                                start=False, stop=True)
                        vt = w.inc("tensor", mmb, "sT")
                    ke = "vector"
                    w.wait(ke, "sT", vt)
                    if ("kvacc", g, sub - 1) in ev:
                        w.wait(ke, "sT", ev[("kvacc", g, sub - 1)])
                    elif sub == 0 and ("kvacc", g - 1, NSUB - 1) in ev:
                        w.wait(ke, "sT",
                               ev[("kvacc", g - 1, NSUB - 1)])
                    if which == "k":
                        def evac(eng, hs=hs):
                            return eng.tensor_scalar(
                                out=kf_tm[:, hs], in0=psC[:, hs],
                                scalar1=EPS_F, scalar2=EPS_F,
                                op0=ALU.add, op1=ALU.max)
                        vv = w.inc("vector", evac, "sV")
                        guardC[half] = ("sV", vv)
                    else:
                        def evac(eng, hs=hs):
                            return eng.tensor_tensor(
                                out=v_tm[:, hs], in0=psC[:, hs],
                                in1=bv_bc[:, hs], op=ALU.add)
                        vv = w.inc("vector", evac, "sV")
                        guardC[half] = ("sV", vv)
                    kv_ev[(which, half)] = vv
                ev[("kvsrc", g, sub, which)] = vv
                if which == "k" and ln_pieces is not None:
                    ln_pieces[sub]()
            first = nglob == 0
            last = nglob == n_grp * NSUB - 1
            for half in range(2):
                w.wait("tensor", "sV", kv_ev[("k", half)])
                w.wait("tensor", "sV", kv_ev[("v", half)])
                if first and half == 0:
                    w.em("tensor", lambda eng, b=b: eng.matmul(
                        psKV[:, :], zb[:, :], lnT[:, b * NCH, :],
                        start=True, stop=False, skip_group_check=True))
                    w.em("tensor", lambda eng, b=b: eng.matmul(
                        psKS[:, 0:NCH], zb[:, :], lnT[:, b * NCH, 0:NCH],
                        start=True, stop=False, skip_group_check=True))
                for h in range(half * 8, half * 8 + 8):
                    par, j = h % 2, h // 2

                    def mm(eng, h=h, par=par, j=j, last=last):
                        return eng.matmul(
                            psKV[64 * par:64 * par + 64,
                                 j * 64:(j + 1) * 64],
                            kf_tm[:, h * 64:(h + 1) * 64],
                            v_tm[:, h * 64:(h + 1) * 64],
                            start=False, stop=last, skip_group_check=True,
                            tile_position=(0, 64 * par))
                    w.em("tensor", mm)
                for cc in range(half * 4, half * 4 + 4):
                    def mm(eng, cc=cc, last=last):
                        return eng.matmul(
                            psKS[:, cc:cc + 1],
                            kf_tm[:, cc * 128:(cc + 1) * 128],
                            ones_c[:, :],
                            start=False, stop=last, skip_group_check=True)
                    if cc == NCH - 1:
                        ev[("kvacc", g, sub)] = w.inc("tensor", mm, "sT")
                    else:
                        w.em("tensor", mm)

        if gi == n_grp - 1:
            w.wait("scalar", "sT", ev[("kvacc", g, NSUB - 1)])
            for h in range(NHEAD):
                par, j = h % 2, h // 2
                rows = slice(64 * par, 64 * par + 64)

                def evac(eng, rows=rows, s=s, j=j, par=par):
                    return eng.copy(
                        kvbd[rows, s, j, 64 * par:64 * par + 64],
                        psKV[rows, j * 64:(j + 1) * 64])
                w.em("scalar", evac)
            for oc in range(NCH):
                def e1(eng, s=s, oc=oc):
                    return eng.copy(ksumb[0:64, s, oc, 2 * oc:2 * oc + 1],
                                    psKS[0:64, oc:oc + 1])

                def e2(eng, s=s, oc=oc):
                    return eng.copy(
                        ksumb[64:128, s, oc, 2 * oc + 1:2 * oc + 2],
                        psKS[64:128, oc:oc + 1])
                w.em("scalar", e1)
                if oc == NCH - 1:
                    ev[("kvbd_evac", s)] = w.inc("scalar", e2, "sS")
                else:
                    w.em("scalar", e2)

    def b_xload(g):
        s, gi = divmod(g, n_grp)
        if g == 0:
            w.wait("sync", *ev[("wprep_cast", 7, 1)])
        elif g == 1:
            w.wait("sync", *ev[("wprep_cast", 7, 0)])
        for sub in range(NSUB):
            if g >= 2:
                w.wait("sync", "sO", 16 * (NSUB * (g - 2) + sub + 1))

            def ld(eng, s=s, gi=gi, g=g, sub=sub):
                return eng.dma_start(
                    xin[:, g % 2, sub, :],
                    x_d[s, gi * GT + sub * 128:gi * GT + (sub + 1) * 128,
                        :])
            ev[("xloadB", g, sub)] = w.dma("sync", ld, "sX")
            w.wait("gpsimd", "sX", ev[("xloadB", g, sub)])

            def boacc(eng, g=g, sub=sub):
                return eng.dma_start(
                    xin[:, g % 2, sub, :],
                    bo_d[:].unsqueeze(0).broadcast_to([128, D]),
                    accum_op=ALU.add)
            ev[("boacc", g, sub)] = w.dma("gpsimd", boacc, "sB")

    def b_qfload(g):
        if ("r_done", g - 2) in ev:
            w.wait("gpsimd", "sT", ev[("r_done", g - 2)])
        w.wait("gpsimd", "sQ", 16 * (g + 1) if g >= 2
               else 16 * (ngroups - 1 + g))

        def ld(eng, g=g):
            return eng.dma_start(
                qf_sb[:, (g % 2) * NCH:(g % 2 + 1) * NCH, :],
                qfs_d[g, :, :, :])
        ev[("qf_load", g)] = w.dma("gpsimd", ld, "sG")

    def dview_of(g):
        return ln_tm[:, g % 2, :, :].rearrange("p a b -> p (a b)").rearrange(
            "p (c d) -> p c d", c=NCH)

    def xkb(g):
        return xk_sb if g % 2 == 0 else lnT

    def b_att(g):
        s, gi = divmod(g, n_grp)
        b = g % 2

        w.wait("tensor", "sG", ev[("qf_load", g)])
        if g == 0:
            w.wait("tensor", "sS", ev[("kvbd_evac", n_seq - 1)])
        if ("recip_eps", g - 1) in ev:
            w.wait("tensor", "sV", ev[("recip_eps", g - 1)])
        for sub in range(NSUB):
            sl = slice(sub * 16, sub * 16 + 16)
            for oc in range(NCH):
                def mm(eng, sl=sl, oc=oc, b=b, sub=sub, s=s):
                    return eng.matmul(
                        psKS[:, sl],
                        qf_sb[:, b * NCH + oc, sub * 128:(sub + 1) * 128],
                        ksumb[:, s, oc, :],
                        start=(oc == 0), stop=(oc == NCH - 1))
                if oc == NCH - 1 and sub == NSUB - 1:
                    vdn = w.inc("tensor", mm, "sT")
                else:
                    w.em("tensor", mm)
        w.wait("vector", "sT", vdn)
        ev[("recip_eps", g)] = w.inc("vector", lambda eng: eng.tensor_scalar(
            out=recTMe[:, :], in0=psKS[:, 0:NSUB * 16], scalar1=EPS_D,
            scalar2=None, op0=ALU.add), "sV")
        vrec = w.inc("vector", lambda eng: eng.reciprocal(
            recTM[:, :], recTMe[:, :]), "sV")
        vh_g = ev.get(("recT_used", g - 1))
        if vh_g is not None:
            w.wait("scalar", "sT", vh_g)
        w.wait("scalar", "sV", vrec)
        vh = w.inc("scalar", lambda eng: eng.copy(recTMh[:, :], recTM[:, :]),
                   "sS")
        if RECIP_HILO:
            if vh_g is not None:
                w.wait("vector", "sT", vh_g)
            w.wait("vector", "sS", vh)
            vlo_ready = w.inc("vector", lambda eng: eng.tensor_tensor(
                out=recTMl[:, :], in0=recTM[:, :], in1=recTMh[:, :],
                op=ALU.subtract), "sV")
            w.wait("tensor", "sV", vlo_ready)
            srcs = (("h", recTMh), ("l", recTMl))
        else:
            srcs = (("h", recTMh),)

        w.wait("tensor", "sS", vh)
        rslots = []
        for which, src in srcs:
            ridx, slot, guard = rotDE.acquire()
            if guard:
                w.wait("tensor", guard[0], guard[1])
            ps = ps_de(slot)
            for sub in range(NSUB):
                def mm(eng, ps=ps, sub=sub, src=src):
                    return eng.matmul(
                        ps[0:16, sub * 128:(sub + 1) * 128],
                        src[:, sub * 16:(sub + 1) * 16], ident[:, :],
                        start=True, stop=True)
                if sub == NSUB - 1:
                    vt = w.inc("tensor", mm, "sT")
                else:
                    w.em("tensor", mm)
            rslots.append((ridx, ps, vt, which))
        ev[("recT_used", g)] = w.c["sT"]
        for ridx, ps, vt, which in rslots:
            w.wait("scalar", "sT", vt)
            dst = recT_h if which == "h" else recT_l

            def evac(eng, ps=ps, dst=dst):
                return eng.copy(dst[:, :], ps[0:16, :])
            vse = w.inc("scalar", evac, "sS")
            rotDE.release(ridx, "sS", vse)
        vre = vse

        w.wait("tensor", "sS", vre)
        for oc in range(NCH):
            ridxb, slotb, guardb = rotAB.acquire()
            if guardb:
                w.wait("tensor", guardb[0], guardb[1])
            psb = ps_ab(slotb)

            def mmh(eng, psb=psb, oc=oc):
                return eng.matmul(psb[:, :], g16[:, oc, :], recT_h[:, :],
                                  start=True, stop=not RECIP_HILO)

            def mml(eng, psb=psb, oc=oc):
                return eng.matmul(psb[:, :], g16[:, oc, :], recT_l[:, :],
                                  start=False, stop=True)
            if RECIP_HILO:
                w.em("tensor", mmh)
                vbc = w.inc("tensor", mml, "sT")
            else:
                vbc = w.inc("tensor", mmh, "sT")

            ridxn, slotn, guardn = rotDE.acquire()
            if guardn:
                w.wait("tensor", guardn[0], guardn[1])
            psn = ps_de(slotn)

            def mmn(eng, psn=psn, oc=oc, b=b, s=s):
                return eng.matmul(psn[:, :], kvbd[:, s, oc, :],
                                  qf_sb[:, b * NCH + oc, :],
                                  start=True, stop=True)
            vnm = w.inc("tensor", mmn, "sT")

            w.wait("scalar", "sT", vbc)
            if ("attn_evac", g, oc - 2) in ev:
                w.wait("scalar", "sV", ev[("attn_evac", g, oc - 2)])
            elif ("ksq", g - 2, oc + 6) in ev:
                w.wait("scalar", "sP" if KSQ_GP else "sV",
                       ev[("ksq", g - 2, oc + 6)])

            def bevac(eng, psb=psb, oc=oc):
                return eng.copy(bc_sb[:, oc % 2, :], psb[:, :])
            vbe = w.inc("scalar", bevac, "sS")
            rotAB.release(ridxb, "sS", vbe)

            w.wait("vector", "sT", vnm)
            w.wait("vector", "sS", vbe)
            if oc == 0 and ("kvmm_done", g - 2) in ev:
                w.wait("vector", "sT", ev[("kvmm_done", g - 2)])

            def aevac(eng, psn=psn, oc=oc):
                return eng.tensor_tensor(out=attn[:, oc, :], in0=psn[:, :],
                                         in1=bc_sb[:, oc % 2, :],
                                         op=ALU.mult)
            vae = w.inc("vector", aevac, "sV")
            ev[("attn_evac", g, oc)] = vae
            rotDE.release(ridxn, "sV", vae)
        ev[("numer_done", g)] = vnm

        if g == 0:
            w.wait("tensor", "sS", ev[("wprep", 3)])
        w.wait("tensor", "sV", ev[("attn_evac", g, NCH - 1)])
        for sub in range(NSUB):
            for half in range(2):
                if guardC[half]:
                    w.wait("tensor", guardC[half][0], guardC[half][1])
                hs = slice(half * 512, (half + 1) * 512)
                for dc in range(NCH):
                    def mm(eng, hs=hs, dc=dc, sub=sub):
                        return eng.matmul(
                            psC[:, hs],
                            attn[:, dc, sub * 128:(sub + 1) * 128],
                            arena[:, 3, dc, hs],
                            start=(dc == 0),
                            stop=(dc == NCH - 1))
                    if dc == NCH - 1:
                        vt = w.inc("tensor", mm, "sT")
                    else:
                        w.em("tensor", mm)
                w.wait("vector", "sT", vt)
                if half == 0:
                    w.wait("vector", "sB", ev[("boacc", g, sub)])

                def x2evac(eng, hs=hs, b=b, sub=sub):
                    return eng.tensor_tensor(
                        out=xin[:, b, sub, hs], in0=psC[:, hs],
                        in1=xin[:, b, sub, hs], op=ALU.add)
                vx2 = w.inc("vector", x2evac, "sV")
                guardC[half] = ("sV", vx2)
                ev[("x2s", g, sub)] = vx2
        ev[("x2", g)] = vx2
        ev[("wo_done", g)] = w.c["sT"]

        ag = None
        if g == 0:
            ag = ("sT", ev[("prep_trmm_b", 0)])
        elif g == 1:
            ag = ("sT", ev[("prep_trmm_b", 1)])
        elif ("mix", g - 2) in ev:
            ag = ("sV", ev[("mix", g - 2)])
        emit_ln(g, "lnB", extra_stat_wait=("sV", ev[("x2", g)]),
                apply_guard=ag)

    def b_tail(g):
        s, gi = divmod(g, n_grp)
        b = g % 2
        if gi == 0:
            if ("col0", g - 1) in ev:
                w.wait("gpsimd", "sS", ev[("col0", g - 1)])
            vbnd = w.inc("gpsimd", lambda eng: eng.memset(bnd[:, :], 0.0),
                         "sP")
        else:
            w.wait("gpsimd", "sS", ev[("tr2_evac", g - 1)])
            w.wait("gpsimd", "sV", ev[("tr2_evac_v", g - 1)])
            if ("col0", g - 1) in ev:
                w.wait("gpsimd", "sS", ev[("col0", g - 1)])
            vbnd = w.inc("gpsimd", lambda eng: eng.tensor_copy(
                bnd[:, :], ln2T[:, :, GT:GT + 1]), "sP")

        def ln2_evac(eng, ps, cc, ve=False):
            if ve:
                return eng.tensor_copy(ln2T[:, cc, 1:GT + 1], ps[:, :])
            return eng.copy(ln2T[:, cc, 1:GT + 1], ps[:, :])
        few = [("sP", vbnd)]
        if ("mix", g - 1) in ev:
            few.append(("sV", ev[("mix", g - 1)]))
        vtr2_s, vtr2_v = emit_transpose(g, 0, ln2_evac, "lnB",
                                        first_evac_waits=few)
        ev[("tr2_evac", g)] = vtr2_s
        ev[("tr2_evac_v", g)] = vtr2_v
        w.wait("scalar", "sP", vbnd)
        w.wait("scalar", "sV", vtr2_v)
        vcol0 = w.inc("scalar", lambda eng: eng.copy(
            ln2T[:, :, 0:1], bnd[:, :]), "sS")
        ev[("col0", g)] = vcol0

        w.wait("vector", "sS", vtr2_s)
        w.wait("vector", "sS", vcol0)
        w.wait("vector", "sT", ev[("wo_done", g)])
        w.wait("vector", "sT", ev[("tr_mm", "lnB", g)])
        xkt = xkb(g)
        dview = dview_of(g)
        w.em("vector", lambda eng, dview=dview: eng.tensor_tensor(
            out=dview[:, :, :], in0=ln2T[:, :, 1:GT + 1],
            in1=ln2T[:, :, 0:GT], op=ALU.subtract))
        for cc in range(NCH):
            def mr(eng, cc=cc, b=b, dview=dview):
                return eng.scalar_tensor_tensor(
                    out=qf_sb[:, b * NCH + cc, :], in0=dview[:, cc, :],
                    scalar=mur[:, cc:cc + 1], in1=ln2T[:, cc, 0:GT],
                    op0=ALU.mult, op1=ALU.add)

            def mk(eng, cc=cc, xkt=xkt, dview=dview):
                return eng.scalar_tensor_tensor(
                    out=xkt[:, cc, :], in0=dview[:, cc, :],
                    scalar=muk[:, cc:cc + 1], in1=ln2T[:, cc, 0:GT],
                    op0=ALU.mult, op1=ALU.add)
            w.em("vector", mr)
            if cc == NCH - 1:
                vmix = w.inc("vector", mk, "sV")
            else:
                w.em("vector", mk)
        ev[("mix", g)] = vmix
        if g >= 1:
            w.em("scalar", lambda eng: eng.activation(
                h05[:, :], eps5[:, :], AF.Sigmoid))

    def b_ffn_k(g):
        s, gi = divmod(g, n_grp)
        b = g % 2
        xkt = xkb(g)
        g_att = g + 1 if g + 1 < ngroups else g

        if g == 0:
            w.wait("tensor", "sS", ev[("wprep", 5)])
            w.wait("tensor", "sV", ev[("wprep_v", 5)])
        w.wait("tensor", "sV", ev[("mix", g)])
        for oc in range(NCH):
            ridx, slot, guard = rotAB.acquire()
            if guard:
                w.wait("tensor", guard[0], guard[1])
            ps = ps_ab(slot)
            for cc in range(NCH):
                def mm(eng, ps=ps, oc=oc, cc=cc, xkt=xkt):
                    return eng.matmul(
                        ps[:, :], arena[:, 0, cc, oc * 128:(oc + 1) * 128],
                        xkt[:, cc, :],
                        start=(cc == 0), stop=(cc == NCH - 1))
                if cc == NCH - 1:
                    vt = w.inc("tensor", mm, "sT")
                else:
                    w.em("tensor", mm)
            w.wait("scalar", "sT", vt)
            if oc < 2:
                w.wait("scalar", "sV", ev[("attn_evac", g_att, oc + 6)])
            else:
                w.wait("scalar", "sP" if KSQ_GP else "sV",
                       ev[("ksq", g, oc - 2)])

            def relu(eng, ps=ps, oc=oc):
                return eng.activation(bc_sb[:, oc % 2, :], ps[:, :], AF.Relu)
            vrl = w.inc("scalar", relu, "sS")
            rotAB.release(ridx, "sS", vrl)
            ksq_e = "gpsimd" if KSQ_GP else "vector"
            ksq_s = "sP" if KSQ_GP else "sV"
            w.wait(ksq_e, "sS", vrl)
            if oc == 0:
                w.wait(ksq_e, "sT", ev[("wo_done", g_att)])

            def sq(eng, oc=oc):
                return eng.tensor_tensor(out=attn[:, oc, :],
                                         in0=bc_sb[:, oc % 2, :],
                                         in1=bc_sb[:, oc % 2, :],
                                         op=ALU.mult)
            ev[("ksq", g, oc)] = w.inc(ksq_e, sq, ksq_s)

    def b_ffn_rest(g):
        s, gi = divmod(g, n_grp)
        b = g % 2

        if g == 0:
            w.wait("tensor", "sS", ev[("wprep", 7)])
            w.wait("tensor", "sV", ev[("wprep_v", 7)])
        if ("tmul_last", g - 1) in ev:
            w.wait("scalar", "sV", ev[("tmul_last", g - 1)])
        for sub in range(NSUB):
            for half in range(2):
                if guardC[half]:
                    w.wait("tensor", guardC[half][0], guardC[half][1])
                hs = slice(half * 512, (half + 1) * 512)
                for cc in range(NCH):
                    def mm(eng, hs=hs, cc=cc, sub=sub, b=b):
                        return eng.matmul(
                            psC[:, hs],
                            qf_sb[:, b * NCH + cc,
                                  sub * 128:(sub + 1) * 128],
                            arena[:, 2, cc, hs],
                            start=(cc == 0), stop=(cc == NCH - 1))
                    if cc == NCH - 1:
                        vt = w.inc("tensor", mm, "sT")
                    else:
                        w.em("tensor", mm)
                w.wait("scalar", "sT", vt)

                def sig(eng, hs=hs, sub=sub):
                    return eng.activation(r_sb[:, sub, hs], psC[:, hs],
                                          AF.Sigmoid)
                vsg = w.inc("scalar", sig, "sS")
                guardC[half] = ("sS", vsg)
        ev[("r_done", g)] = w.c["sT"]
        if g + 2 < ngroups:
            b_qfload(g + 2)

        if g == 0:
            wp6 = ev[("wprep", 6)]
            if isinstance(wp6, tuple):
                w.wait("tensor", wp6[0], wp6[1])
            else:
                w.wait("tensor", "sS", wp6)
            w.wait("tensor", "sV", ev[("wprep_v", 6)])
        w.wait("tensor", "sP" if KSQ_GP else "sV",
               ev[("ksq", g, NCH - 1)])
        vtm_last = None
        for sub in range(NSUB):
            for half in range(2):
                if guardC[half]:
                    w.wait("tensor", guardC[half][0], guardC[half][1])
                hs = slice(half * 512, (half + 1) * 512)
                for cc in range(NCH):
                    def mm(eng, hs=hs, cc=cc, sub=sub):
                        return eng.matmul(
                            psC[:, hs],
                            attn[:, cc, sub * 128:(sub + 1) * 128],
                            arena[:, 1, cc, hs],
                            start=(cc == 0), stop=(cc == NCH - 1))
                    if cc == NCH - 1:
                        vt = w.inc("tensor", mm, "sT")
                    else:
                        w.em("tensor", mm)
                w.wait("vector", "sT", vt)

                def tmul(eng, hs=hs, sub=sub):
                    return eng.tensor_tensor(
                        out=r_sb[:, sub, hs], in0=psC[:, hs],
                        in1=r_sb[:, sub, hs], op=ALU.mult)
                vtm = w.inc("vector", tmul, "sV")
                guardC[half] = ("sV", vtm)
                vtm_last = vtm

            def fin(eng, sub=sub, b=b):
                return eng.tensor_tensor(
                    out=xin[:, b, sub, :], in0=r_sb[:, sub, :],
                    in1=xin[:, b, sub, :], op=ALU.add)
            vfin = w.inc("vector", fin, "sV")
            w.wait("gpsimd", "sV", vfin)

            def st(eng, s=s, gi=gi, sub=sub, b=b):
                return eng.dma_start(
                    out_d[s, gi * GT + sub * 128:gi * GT + (sub + 1) * 128,
                          :], xin[:, b, sub, :])
            w.dma("gpsimd", st, "sO")
        ev[("kvmm_done", g)] = w.c["sT"]
        ev[("tmul_last", g)] = vtm_last
        w.em("scalar", lambda eng: eng.activation(
            h05[:, :], eps5[:, :], AF.Sqrt))
        if g + 2 < ngroups:
            b_xload(g + 2)

    attn_scr = attn[:, :, :].rearrange("p a b -> p (a b)")
    prep_weight(0, 0, wds["wq"], True, 0, scratch=attn_scr)
    prep_weight(1, 1, wds["wk"], True, 1)
    prep_weight(2, 2, wds["wv"], False, 0,
                dma_waits=[ev[("wprep_cast", 0, 1)]],
                scratch=attn_scr)
    prep_weight(3, 3, wds["wo"], False, 1,
                dma_waits=[ev[("wprep_cast", 1, 1)]])

    a_xload(0)
    a_xload(1)
    emit_ln_a(0)
    if ngroups > 1:
        emit_ln_a(1)
    for g in range(ngroups):
        phase_a_group(g)

    b_qfload(0)
    b_qfload(1)
    prep_weight_b(5, 0, wds["wk2"],
                  dma_waits_h=([("sV", ev[("lnA", ngroups - 1)])],
                               [("sV", ev[("lnA", ngroups - 2)])]),
                  cast_waits_h=(
                      [("sT", ev[("tr_mm", "lnA", ngroups - 2)]),
                       ("sT", ev[("prep_trmm", "ln_tm")])],
                      [("sT", ev[("tr_mm", "lnA", ngroups - 1)])]))
    prep_weight_b(6, 1, wds["wv2"],
                  dma_waits_h=([ev[("wprep_cast", 5, 0)]],
                               [ev[("wprep_cast", 5, 1)]]))
    prep_weight_b(7, 2, wds["wr"],
                  dma_waits_h=([ev[("wprep_cast", 6, 0)]],
                               [ev[("wprep_cast", 6, 1)]]))

    b_xload(0)
    b_xload(1)
    for g in range(ngroups):
        b_att(g)
        if g >= 1:
            b_ffn_k(g - 1)
        b_tail(g)
        if g >= 1:
            b_ffn_rest(g - 1)
    b_ffn_k(ngroups - 1)
    b_ffn_rest(ngroups - 1)

    w.wait("gpsimd", "sO", 16 * NSUB * ngroups)

    if debug:
        dbg_names = ["stats", "ln_tm", "lnT", "kf_tm", "v_tm", "kvbd",
                     "ksumb", "qf_sb", "recTM", "recT_h", "recT_l", "attn",
                     "xk_sb", "r_sb", "ln2T", "t_sb", "bc_sb", "arena",
                     "cqe", "ckrow", "bvrow", "bo_hi", "bo_lo", "pbf",
                     "ident", "g16", "mur_s", "muk_s"]
        dbg_map = {"stats": stats, "ln_tm": ln_tm, "lnT": lnT,
                   "kf_tm": kf_tm, "v_tm": v_tm, "kvbd": kvbd,
                   "ksumb": ksumb, "qf_sb": qf_sb, "recTM": recTM,
                   "recT_h": recT_h, "recT_l": recT_l, "attn": attn,
                   "xk_sb": xk_sb, "r_sb": r_sb, "ln2T": ln2T,
                   "t_sb": t_sb, "bc_sb": bc_sb, "arena": arena,
                   "cqe": cqe, "ckrow": ckrow, "bvrow": bvrow,
                   "bo_hi": bo_hi, "bo_lo": bo_lo, "pbf": pbf,
                   "ident": ident, "g16": g16, "mur_s": mur, "muk_s": muk}
        for nm in dbg_names:
            t = dbg_map[nm]
            shp = list(t.shape) if hasattr(t, "shape") else None
            d = nc.dram_tensor("dbg_" + nm, shp, t.dtype,
                               kind="ExternalOutput")
            full = t[tuple(slice(None) for _ in shp)]
            w.dma("gpsimd", lambda eng, d=d, full=full, shp=shp:
                  eng.dma_start(
                      d[tuple(slice(None) for _ in shp)], full), "sO")
        w.wait("gpsimd", "sO", w.c["sO"])

    with nc.Block() as blk:

        @blk.sync
        def _(eng):
            for f in w.q["sync"]:
                f(eng)

        @blk.tensor
        def _(eng):
            for f in w.q["tensor"]:
                f(eng)

        @blk.vector
        def _(eng):
            for f in w.q["vector"]:
                f(eng)

        @blk.scalar
        def _(eng):
            for f in w.q["scalar"]:
                f(eng)

        @blk.gpsimd
        def _(eng):
            for f in w.q["gpsimd"]:
                f(eng)

    return nc


def _gsel_const():
    g = np.zeros((16, NCH, 128), np.float32)
    for oc in range(NCH):
        g[2 * oc, oc, 0:64] = 1.0
        g[2 * oc + 1, oc, 64:128] = 1.0
    return g


def kernel(L0_attn_x, L0_attn_Wq, L0_attn_bq, L0_attn_Wk, L0_attn_bk,
           L0_attn_Wv, L0_attn_bv, L0_attn_Wo, L0_attn_bo, L0_attn_P,
           L0_ffn_W_r, L0_ffn_W_k, L0_ffn_W_v, L0_ffn_mu_r, L0_ffn_mu_k):
    x = np.ascontiguousarray(np.asarray(L0_attn_x, dtype=np.float32))
    B = x.shape[0]
    n_cores = 8
    per = B // n_cores
    nc = build_program(n_seq=per, n_grp=x.shape[1] // GT)
    common = dict(
        wq=np.ascontiguousarray(np.asarray(L0_attn_Wq, np.float32)),
        wk=np.ascontiguousarray(np.asarray(L0_attn_Wk, np.float32)),
        wv=np.ascontiguousarray(np.asarray(L0_attn_Wv, np.float32)),
        wo=np.ascontiguousarray(np.asarray(L0_attn_Wo, np.float32)),
        wr=np.ascontiguousarray(np.asarray(L0_ffn_W_r, np.float32)),
        wk2=np.ascontiguousarray(np.asarray(L0_ffn_W_k, np.float32)),
        wv2=np.ascontiguousarray(np.asarray(L0_ffn_W_v, np.float32)),
        bq=np.ascontiguousarray(np.asarray(L0_attn_bq, np.float32)),
        bk=np.ascontiguousarray(np.asarray(L0_attn_bk, np.float32)),
        bv=np.ascontiguousarray(np.asarray(L0_attn_bv, np.float32)),
        bo=np.ascontiguousarray(np.asarray(L0_attn_bo, np.float32)),
        p=np.ascontiguousarray(np.asarray(L0_attn_P, np.float32)),
        mur=np.ascontiguousarray(np.asarray(L0_ffn_mu_r, np.float32)),
        muk=np.ascontiguousarray(np.asarray(L0_ffn_mu_k, np.float32)),
        gsel=_gsel_const(),
    )
    in_maps = []
    for c in range(n_cores):
        m = dict(common)
        m["x"] = np.ascontiguousarray(x[c * per:(c + 1) * per])
        in_maps.append(m)
    res = run_bass_kernel_spmd(nc, in_maps, core_ids=list(range(n_cores)))
    return np.concatenate([r["out"] for r in res.results], axis=0)
